# version 02
# baseline (speedup 1.0000x reference)
import numpy as np
import ml_dtypes
from contextlib import ExitStack

import concourse.bass as bass
import concourse.mybir as mybir
from concourse.bass_utils import run_bass_kernel_spmd

F32 = mybir.dt.float32
BF16 = mybir.dt.bfloat16
I32 = mybir.dt.int32
AF = mybir.ActivationFunctionType
ALU = mybir.AluOpType
AX = mybir.AxisListType

NCORE = 8
B_FULL, S, D = 16, 2048, 1024
SEQ = 2
DC = D // 128
QT = 512
NQT = S // QT
KC = S // 128
C_SOFT = 20.0
LN_EPS = 1e-5
RMS_EPS = 1e-6

DEV = False

QUEUES = ("t", "v", "s", "g", "y")

ALL_PHASES = ["ln0", "mod0", "attn0", "ca0", "camix", "mc0", "mcmix", "wo0",
              "rms1", "ropeattn", "ropeproj", "rms2", "comb", "moe",
              "ln1", "attn1", "wo1", "ffn1"]


class Builder:
    def __init__(self, nc):
        self.nc = nc
        self.recs = {q: [] for q in QUEUES}
        self.ctx = ExitStack()
        self._sems = {}
        self.ctr = {}
        self.phase_touched = {}
        self.phase_names = []

    def sem(self, name):
        if name not in self._sems:
            self._sems[name] = self.ctx.enter_context(self.nc.semaphore(name))
            self.ctr[name] = 0
        return self._sems[name]

    def op(self, q, method, *args, inc=None, lp=False, **kw):
        incs = []
        if inc is not None:
            sname, n = inc
            self.sem(sname)
            self.ctr[sname] += n
            self.phase_touched[sname] = self.ctr[sname]
            incs.append((sname, n))
        if lp:
            kw["_lp"] = True
        self.recs[q].append((method, args, kw, incs))

    def wait(self, q, sname, val=None):
        self.sem(sname)
        if val is None:
            val = self.ctr[sname]
        if val > 0:
            self.recs[q].append(("wait_ge", (self._sems[sname], val), {}, []))

    def dma(self, q, out, in_, sname, **kw):
        self.op(q, "dma_start", inc=(sname, 16), out=out, in_=in_, **kw)

    def mark(self, sname):
        self.sem(sname)
        return self.ctr[sname]

    def phase(self, name=None):
        touched = {k: v for k, v in self.phase_touched.items() if v > 0}
        for q in QUEUES:
            for sname, val in touched.items():
                self.wait(q, sname, val)
        self.phase_touched = {}
        if name:
            self.phase_names.append(name)

    def sbuf(self, name, shape, dt, scope=None):
        scope = scope if scope is not None else self.ctx
        return scope.enter_context(self.nc.sbuf_tensor(name, shape, dt))

    def psum(self, name, shape, dt):
        return self.ctx.enter_context(self.nc.psum_tensor(name, shape, dt))

    def finalize(self):
        blk = self.ctx.enter_context(self.nc.Block())
        sems = self._sems

        def mk(q):
            def runner(eng):
                for ri, (method, args, kw, incs) in enumerate(self.recs[q]):
                    try:
                        if kw.get("_lp"):
                            kw2 = {k: v for k, v in kw.items() if k != "_lp"}
                            with self.nc.allow_low_precision(reason="intentional bf16 output"):
                                inst = getattr(eng, method)(*args, **kw2)
                        else:
                            inst = getattr(eng, method)(*args, **kw)
                    except Exception as e:
                        raise RuntimeError(f"queue {q} rec {ri} {method} "
                                           f"kw={list(kw)}: {e}") from e
                    for sname, n in incs:
                        inst.then_inc(sems[sname], n)
            return runner

        for q, attr in (("t", "tensor"), ("v", "vector"), ("s", "scalar"),
                        ("g", "gpsimd"), ("y", "sync")):
            if self.recs[q]:
                getattr(blk, attr)(mk(q))

    def n_instr(self):
        return {q: len(r) for q, r in self.recs.items()}


WEIGHT_SPECS = [
    ("xT", [SEQ, D, S], F32),
    ("l0a_wq", [D, D], F32), ("l0a_bq", [D], F32),
    ("l0a_wk", [D, D], F32), ("l0a_bk", [D], F32),
    ("l0a_wv", [D, D], F32), ("l0a_bv", [D], F32),
    ("l0a_cgw", [D, 16], F32), ("l0a_cgb", [16], F32),
    ("l0a_amw", [16, 16], F32), ("l0a_amb", [16], F32),
    ("l0a_cons", [16], F32),
    ("l0a_cawin", [D, 3 * D], F32), ("l0a_cabin", [3 * D], F32),
    ("l0a_cawout", [D, D], F32), ("l0a_cabout", [D], F32),
    ("l0a_mcwin", [D, 3 * D], F32), ("l0a_mcbin", [3 * D], F32),
    ("l0a_mcwout", [D, D], F32), ("l0a_mcbout", [D], F32),
    ("l0a_wo", [D, D], F32), ("l0a_bo", [D], F32),
    ("l0f_n1w", [D], F32), ("l0f_n2w", [D], F32),
    ("l0f_qkvwT", [D, 3 * D], F32), ("l0f_qkvb", [3 * D], F32),
    ("l0f_outwT", [D, D], F32), ("l0f_outb", [D], F32),
    ("l0f_gatewT", [D, 8], F32),
    ("l0f_w1T", [8, D, 2048], F32), ("l0f_b1", [8, 2048], F32),
    ("l0f_w2T", [8, D, 2048], F32), ("l0f_b2", [8, 2048], F32),
    ("l0f_w3T", [8, 2048, D], F32), ("l0f_b3", [8, D], F32),
    ("l1a_wqT", [D, D], F32), ("l1a_bq", [D], F32),
    ("l1a_wkT", [D, D], F32), ("l1a_bk", [D], F32),
    ("l1a_wvT", [D, D], F32), ("l1a_bv", [D], F32),
    ("l1a_woT", [D, D], F32), ("l1a_bo", [D], F32),
    ("l1f_wgT", [D, 3 * D], I32), ("l1f_wuT", [D, 3 * D], I32),
    ("l1f_wdT", [3 * D, D], I32),
    ("l0a_bq_c", [128, 8], F32), ("l0a_bk_c", [128, 8], F32),
    ("l0a_cabin_c", [128, 24], F32), ("l0a_mcbin_c", [128, 24], F32),
    ("l0a_cabout_c", [128, 8], F32), ("l0a_mcbout_c", [128, 8], F32),
    ("l0a_bo_c", [128, 8], F32),
    ("l0f_qkvb_c", [128, 24], F32), ("l0f_outb_c", [128, 8], F32),
    ("l0f_b1_c", [8, 128, 16], F32), ("l0f_b2_c", [8, 128, 16], F32),
    ("l0f_b3_c", [8, 128, 8], F32),
    ("l0f_n1w_c", [128, 8], F32), ("l0f_n2w_c", [128, 8], F32),
    ("l1a_bq_c", [128, 8], F32), ("l1a_bk_c", [128, 8], F32),
    ("l1a_bo_c", [128, 8], F32),
]

SCRATCH_SPECS = [
    ("xn_h", [SEQ, D, S], BF16),
    ("s_h", [SEQ, D, S], BF16),
    ("x1_h", [SEQ, D, S], BF16),
    ("y_h", [SEQ, D, S], BF16),
    ("ctxa_h", [SEQ, D, S], BF16),
    ("ctxb_h", [SEQ, D, S], BF16),
    ("acc_h", [SEQ, D, S], F32),
    ("xr_h", [SEQ, D, S], F32),
    ("combT_h", [SEQ, 8, S], F32),
]


def declare_hbm(nc):
    H = {}
    for name, shape, dt in WEIGHT_SPECS:
        H[name] = nc.dram_tensor(name, shape, dt, kind="ExternalInput").ap()
    H["outT"] = nc.dram_tensor("outT", [SEQ, D, S], F32, kind="ExternalOutput").ap()
    skind = "ExternalOutput" if DEV else "Internal"
    for name, shape, dt in SCRATCH_SPECS:
        H[name] = nc.dram_tensor(name, shape, dt, kind=skind).ap()
    return H


class Consts:
    pass


def emit_consts(B, PS):
    C = Consts()
    C.ident = B.sbuf("c_ident", [128, 128], BF16)
    C.ones_row_bf = B.sbuf("c_ones_bf", [1, 128], BF16)
    C.ones_row_f = B.sbuf("c_ones_f", [1, 128], F32)
    C.ones_col_bf = B.sbuf("c_onescol_bf", [128, 1], BF16)
    C.ones_col_f = B.sbuf("c_onescol_f", [128, 1], F32)
    C.negc = B.sbuf("c_negc", [128, 1], F32)
    C.eps_ln = B.sbuf("c_epsln", [1, 1], F32)
    C.eps_rms = B.sbuf("c_epsrms", [1, 1], F32)
    C.pairsel = [B.sbuf(f"c_pairsel{hp}", [16, 128], BF16) for hp in range(8)]
    C.esel = [B.sbuf(f"c_esel{e}", [8, 128], BF16) for e in range(8)]
    C.hsel2 = B.sbuf("c_hsel2", [2, 128], BF16)
    C.rcos = B.sbuf("c_rcos", [64, S], BF16)
    C.rsin = B.sbuf("c_rsin", [64, S], BF16)

    with ExitStack() as sc:
        it_pf = B.sbuf("c_it_pf", [128, 128], I32, sc)
        it_sel = B.sbuf("c_it_sel", [16, 128], I32, sc)
        it_e = B.sbuf("c_it_e", [8, 128], I32, sc)
        it_inv = B.sbuf("c_it_inv", [1, 64], I32, sc)
        it_pos = B.sbuf("c_it_pos", [1, S], I32, sc)
        f_inv = B.sbuf("c_f_inv", [1, 64], F32, sc)
        f_pos = B.sbuf("c_f_pos", [1, S], F32, sc)
        signc = B.sbuf("c_signc", [64, 1], F32, sc)
        th_f = B.sbuf("c_th_f", [64, 512], F32, sc)
        th_i = B.sbuf("c_th_i", [64, 512], I32, sc)
        th_r = B.sbuf("c_th_r", [64, 512], F32, sc)

        B.op("g", "iota", it_pf[:], [[1, 128]], base=0, channel_multiplier=-1, inc=("cst", 1))
        B.op("g", "iota", it_sel[:], [[-1, 2], [0, 64]], base=0, channel_multiplier=1, inc=("cst", 1))
        B.op("g", "iota", it_e[:], [[0, 128]], base=0, channel_multiplier=1, inc=("cst", 1))
        B.op("g", "iota", it_inv[:], [[0, 2], [1, 32]], channel_multiplier=0, inc=("cst", 1))
        B.op("g", "iota", it_pos[:], [[1, S]], channel_multiplier=0, inc=("cst", 1))
        B.op("g", "memset", C.ones_row_bf[:], 1.0, inc=("cst", 1))
        B.op("g", "memset", C.ones_row_f[:], 1.0, inc=("cst", 1))
        B.op("g", "memset", C.ones_col_bf[:], 1.0, inc=("cst", 1))
        B.op("g", "memset", C.ones_col_f[:], 1.0, inc=("cst", 1))
        B.op("g", "memset", C.negc[:], -C_SOFT, inc=("cst", 1))
        B.op("g", "memset", C.eps_ln[:], LN_EPS, inc=("cst", 1))
        B.op("g", "memset", C.eps_rms[:], RMS_EPS, inc=("cst", 1))
        B.op("g", "memset", signc[:], 1.0, inc=("cst", 1))
        B.op("g", "memset", signc[32:64, :], -1.0, inc=("cst", 1))

        B.wait("v", "cst")
        B.op("v", "tensor_single_scalar", C.ident[:], it_pf[:], 0.0, ALU.is_equal, inc=("cstv", 1))
        for hp in range(8):
            B.op("v", "tensor_single_scalar", C.pairsel[hp][:], it_sel[:], float(2 * hp), ALU.is_equal, inc=("cstv", 1))
            B.wait("v", "cstv")
            B.op("v", "tensor_scalar_mul", C.pairsel[hp][:], C.pairsel[hp][:], 0.125, inc=("cstv", 1))
        for e in range(8):
            B.op("v", "tensor_single_scalar", C.esel[e][:], it_e[:], float(e), ALU.is_equal, inc=("cstv", 1))
        B.op("v", "tensor_single_scalar", C.hsel2[:], it_sel[0:2, :], 0.0, ALU.is_equal, inc=("cstv", 1))
        B.op("v", "tensor_copy", f_inv[:], it_inv[:], inc=("cstv", 1))
        B.op("v", "tensor_copy", f_pos[:], it_pos[:], inc=("cstv", 1))

        B.wait("s", "cstv")
        B.op("s", "activation", f_inv[:], f_inv[:], AF.Exp,
             scale=float(-np.log(10000.0) / 32.0), inc=("cstf", 1))

        for qt in range(NQT):
            ts = slice(qt * 512, (qt + 1) * 512)
            B.wait("t", "cstf")
            if qt > 0:
                B.wait("t", "rope_rd", B.mark("rope_rd"))
            B.op("t", "matmul", PS.b[0][0:64, :], f_inv[:], f_pos[:, ts],
                 start=True, stop=True, inc=("rope_mm", 1))
            for is_cos in (0, 1):
                B.wait("v", "rope_mm", B.mark("rope_mm"))
                if is_cos == 0:
                    B.op("v", "tensor_scalar_mul", th_f[:], PS.b[0][0:64, :],
                         float(1.0 / (2 * np.pi)), inc=("rope_v", 1))
                else:
                    B.op("v", "tensor_scalar", th_f[:], PS.b[0][0:64, :],
                         float(1.0 / (2 * np.pi)), 0.25, ALU.mult, ALU.add, inc=("rope_rd", 1))
                B.op("v", "tensor_copy", th_i[:], th_f[:])
                B.op("v", "tensor_copy", th_r[:], th_i[:])
                B.op("v", "tensor_tensor", th_r[:], th_f[:], th_r[:], ALU.subtract)
                B.op("v", "tensor_scalar_mul", th_r[:], th_r[:], float(2 * np.pi), inc=("rope_v", 1))
                B.wait("s", "rope_v", B.mark("rope_v"))
                dst = (C.rcos if is_cos else C.rsin)[:, ts]
                B.op("s", "activation", dst, th_r[:], AF.Sin, inc=("rope_s", 1))
                B.wait("v", "rope_s", B.mark("rope_s"))
        B.op("v", "tensor_scalar_mul", C.rsin[:], C.rsin[:], signc[:, 0:1], inc=("cstv", 1))
    return C


def emit_layernorm(B, C, PS, src_h, dst_h, tag):
    with ExitStack() as sc:
        xf = [B.sbuf(f"{tag}_xf{c}", [128, QT], F32, sc) for c in range(DC)]
        xb = [B.sbuf(f"{tag}_xb{c}", [128, QT], BF16, sc) for c in range(DC)]
        sq = [B.sbuf(f"{tag}_sq{c}", [128, QT], BF16, sc) for c in range(DC)]
        r_mean = B.sbuf(f"{tag}_rmean", [1, QT], F32, sc)
        r_msq = B.sbuf(f"{tag}_rmsq", [1, QT], F32, sc)
        r_var = B.sbuf(f"{tag}_rvar", [1, QT], F32, sc)
        r_std = B.sbuf(f"{tag}_rstd", [1, QT], F32, sc)
        r_rstd = B.sbuf(f"{tag}_rrstd", [1, QT], F32, sc)
        rmean_b = B.sbuf(f"{tag}_rmeanb", [1, QT], BF16, sc)
        rrstd_b = B.sbuf(f"{tag}_rrstdb", [1, QT], BF16, sc)
        tmp = [B.sbuf(f"{tag}_tmp{i}", [128, QT], F32, sc) for i in range(2)]
        xout = [B.sbuf(f"{tag}_xo{i}", [128, QT], BF16, sc) for i in range(2)]
        ld, sqd, mmd, rowd, bcd, evd, spl = "ld", "sq", "mm", "row", "bc", "ev", "sp"
        spl_marks = []
        it = 0
        for seq in range(SEQ):
            for qt in range(NQT):
                ts = slice(qt * QT, (qt + 1) * QT)
                if it > 0:
                    B.wait("y", evd, B.mark(evd))
                for c in range(DC):
                    B.dma("y", xf[c][:], src_h[seq, c * 128:(c + 1) * 128, ts], ld)
                ldv = B.mark(ld)
                B.wait("s", ld, ldv)
                for c in range(DC):
                    B.op("s", "activation", sq[c][:], xf[c][:], AF.Square, inc=(sqd, 1))
                B.wait("v", ld, ldv)
                for c in range(DC):
                    B.op("v", "tensor_copy", xb[c][:], xf[c][:], inc=(sqd, 1))
                B.wait("t", sqd, B.mark(sqd))
                if it > 0:
                    B.wait("t", rowd, B.mark(rowd))
                for c in range(DC):
                    B.op("t", "matmul", PS.A[0][0:1, 0:512], C.ones_col_bf[:], xb[c][:],
                         start=(c == 0), stop=(c == DC - 1),
                         inc=((mmd, 1) if c == DC - 1 else None))
                for c in range(DC):
                    B.op("t", "matmul", PS.A[0][0:1, 512:1024], C.ones_col_bf[:], sq[c][:],
                         start=(c == 0), stop=(c == DC - 1),
                         inc=((mmd, 1) if c == DC - 1 else None))
                B.wait("v", mmd, B.mark(mmd))
                B.op("v", "tensor_scalar_mul", r_mean[:], PS.A[0][0:1, 0:512], float(1.0 / D))
                B.op("v", "tensor_scalar_mul", r_msq[:], PS.A[0][0:1, 512:1024], float(1.0 / D))
                B.op("v", "tensor_tensor", r_var[:], r_mean[:], r_mean[:], ALU.mult)
                B.op("v", "tensor_tensor", r_var[:], r_msq[:], r_var[:], ALU.subtract, inc=(rowd, 1))
                B.wait("s", rowd, B.mark(rowd))
                B.op("s", "activation", r_std[:], r_var[:], AF.Sqrt, bias=C.eps_ln[:, 0:1], inc=(rowd, 1))
                B.wait("v", rowd, B.mark(rowd))
                B.op("v", "reciprocal", r_rstd[:], r_std[:], inc=(rowd, 1))
                B.wait("v", rowd, B.mark(rowd))
                B.op("v", "tensor_copy", rmean_b[:], r_mean[:])
                B.op("v", "tensor_copy", rrstd_b[:], r_rstd[:], inc=(rowd, 1))
                B.wait("t", rowd, B.mark(rowd))
                if it > 0:
                    B.wait("t", evd, B.mark(evd))
                B.op("t", "matmul", PS.b[0][:], C.ones_row_bf[0:1, :], rmean_b[:], start=True, stop=True, inc=(bcd, 1))
                B.op("t", "matmul", PS.b[1][:], C.ones_row_bf[0:1, :], rrstd_b[:], start=True, stop=True, inc=(bcd, 1))
                B.wait("v", bcd, B.mark(bcd))
                for c in range(DC):
                    buf = c % 2
                    if len(spl_marks) >= 2:
                        B.wait("v", spl, spl_marks[-2])
                    B.op("v", "tensor_tensor", tmp[buf][:], xf[c][:], PS.b[0][:], ALU.subtract)
                    B.op("v", "tensor_tensor", xout[buf][:], tmp[buf][:], PS.b[1][:], ALU.mult, inc=(evd, 1))
                    B.wait("y", evd, B.mark(evd))
                    B.dma("y", dst_h[seq, c * 128:(c + 1) * 128, ts], xout[buf][:], spl)
                    spl_marks.append(B.mark(spl))
                it += 1


def col_f32(ap1d, a, b):
    return ap1d[a:b].rearrange("(p o) -> p o", o=1)


def row_f32(ap1d, a, b):
    return ap1d[a:b].rearrange("(o p) -> o p", o=1)


def emit_mod(B, C, PS, H, modT_sb):
    with ExitStack() as sc:
        cgw = B.sbuf("md_cgw", [128, 8 * 16], BF16, sc)
        amw = B.sbuf("md_amw", [16, 16], BF16, sc)
        cons = B.sbuf("md_cons", [16, 1], BF16, sc)
        cgb = B.sbuf("md_cgb", [16, 1], F32, sc)
        amb = B.sbuf("md_amb", [16, 1], F32, sc)
        biasc = B.sbuf("md_biasc", [16, 1], F32, sc)
        xt = [B.sbuf(f"md_x{c}", [128, QT], BF16, sc) for c in range(DC)]
        for c in range(DC):
            B.dma("g", cgw[:, c * 16:(c + 1) * 16], H["l0a_cgw"][c * 128:(c + 1) * 128, :], "mdw")
        B.dma("g", amw[:], H["l0a_amw"], "mdw")
        B.dma("g", cons[:], col_f32(H["l0a_cons"], 0, 16), "mdw")
        B.dma("y", cgb[:], col_f32(H["l0a_cgb"], 0, 16), "mdw")
        B.dma("y", amb[:], col_f32(H["l0a_amb"], 0, 16), "mdw")
        wv_ = B.mark("mdw")
        B.wait("t", "mdw", wv_)
        B.op("t", "matmul", PS.b[2][0:16, 0:1], amw[:], cons[:], start=True, stop=True, inc=("mdm", 1))
        B.wait("v", "mdm", B.mark("mdm"))
        B.op("v", "tensor_tensor", biasc[:], PS.b[2][0:16, 0:1], cgb[:], ALU.add)
        B.op("v", "tensor_tensor", biasc[:], biasc[:], amb[:], ALU.add, inc=("mdv", 1))
        for seq in range(SEQ):
            for qt in range(NQT):
                ts = slice(qt * QT, (qt + 1) * QT)
                it = seq * NQT + qt
                if it > 0:
                    B.wait("y", "mds", B.mark("mds"))
                for c in range(DC):
                    B.dma("y", xt[c][:], H["xn_h"][seq, c * 128:(c + 1) * 128, ts], "mdx")
                B.wait("t", "mdx", B.mark("mdx"))
                if it > 0:
                    B.wait("t", "mds", B.mark("mds"))
                for c in range(DC):
                    B.op("t", "matmul", PS.b[0][0:16, :], cgw[:, c * 16:(c + 1) * 16], xt[c][:],
                         start=(c == 0), stop=(c == DC - 1),
                         inc=(("mdm", 1) if c == DC - 1 else None))
                B.wait("s", "mdm", B.mark("mdm"))
                B.wait("s", "mdv", 1)
                B.op("s", "activation", modT_sb[seq][:, ts], PS.b[0][0:16, :], AF.Sigmoid,
                     bias=biasc[:, 0:1], inc=("mds", 1))


def emit_attn_pairs(B, C, PS, H, tag, seq, src, wq, bq_c, wk, bk_c, wv, bv,
                    ctx_dst, modT=None, rope=False, exp_scale=0.125):
    with ExitStack() as sc:
        xn_sb = B.sbuf(f"{tag}_xn", [128, 16384], BF16, sc)
        vt_sb = B.sbuf(f"{tag}_vt", [128, 16384], BF16, sc)
        wv_sb = B.sbuf(f"{tag}_wv", [128, 8192], BF16, sc)
        wqp = B.sbuf(f"{tag}_wqp", [128, 1024], BF16, sc)
        wkp = B.sbuf(f"{tag}_wkp", [128, 1024], BF16, sc)
        qtp = B.sbuf(f"{tag}_qtp", [128, S], BF16, sc)
        ktp = B.sbuf(f"{tag}_ktp", [128, S], BF16, sc)
        probs = B.sbuf(f"{tag}_pr", [128, 16384], BF16, sc)
        bvbc = B.sbuf(f"{tag}_bvbc", [128, 1024], F32, sc)
        bvrow = B.sbuf(f"{tag}_bvrow", [1, 1024], BF16, sc)
        bqc8 = B.sbuf(f"{tag}_bqc", [128, 8], F32, sc)
        bkc8 = B.sbuf(f"{tag}_bkc", [128, 8], F32, sc)
        modbc = B.sbuf(f"{tag}_modbc", [128, QT], F32, sc)
        qtmp = B.sbuf(f"{tag}_qtmp", [128, QT], F32, sc)
        rec = [B.sbuf(f"{tag}_rec{h}", [1, QT], BF16, sc) for h in range(2)]
        recbc = B.sbuf(f"{tag}_recbc", [128, QT], F32, sc)
        ctxo = B.sbuf(f"{tag}_ctxo", [128, QT], BF16, sc)
        rst = [B.sbuf(f"{tag}_rst{i}", [64, QT], F32, sc) for i in range(3)]

        ld, wl, mm, ex, pv, nr, ev, sp = "ld", "wl", "mm", "ex", "pv", "nr", "ev", "sp"
        if modT is not None:
            exp_scale = 1.0
        for c in range(DC):
            for qq in range(NQT):
                B.dma("y", xn_sb[:, c * S + qq * QT:c * S + (qq + 1) * QT],
                      src[seq, c * 128:(c + 1) * 128, qq * QT:(qq + 1) * QT], ld)
        for c in range(DC):
            B.dma("g", wv_sb[:, c * 1024:(c + 1) * 1024], wv[c * 128:(c + 1) * 128, :], wl)
        B.dma("g", bvrow[:], row_f32(bv, 0, 1024), wl)
        B.dma("y", bqc8[:], bq_c, wl)
        B.dma("y", bkc8[:], bk_c, wl)
        B.wait("t", wl, B.mark(wl))
        B.op("t", "matmul", PS.b[0][:], C.ones_row_bf[:], bvrow[:, 0:512], start=True, stop=True, inc=(mm, 1))
        B.op("t", "matmul", PS.b[1][:], C.ones_row_bf[:], bvrow[:, 512:1024], start=True, stop=True, inc=(mm, 1))
        B.wait("s", mm, B.mark(mm))
        B.op("s", "copy", bvbc[:, 0:512], PS.b[0][:], inc=(nr, 1))
        B.op("s", "copy", bvbc[:, 512:1024], PS.b[1][:], inc=(nr, 1))
        B.wait("t", ld, B.mark(ld))
        B.wait("t", nr, B.mark(nr))
        vev = []
        vi = 0
        for tt in range(16):
            for grp in range(2):
                bank = PS.b[vi % 2]
                if vi >= 2:
                    B.wait("t", ev, vev[vi - 2])
                for c in range(DC):
                    B.op("t", "matmul", bank[:],
                         xn_sb[:, c * S + tt * 128:c * S + (tt + 1) * 128],
                         wv_sb[:, c * 1024 + grp * 512:c * 1024 + (grp + 1) * 512],
                         start=(c == 0), stop=(c == DC - 1),
                         inc=((mm, 1) if c == DC - 1 else None))
                B.wait("v", mm, B.mark(mm))
                B.op("v", "tensor_tensor",
                     vt_sb[:, tt * 1024 + grp * 512:tt * 1024 + (grp + 1) * 512],
                     bank[:], bvbc[:, grp * 512:(grp + 1) * 512], ALU.add, inc=(ev, 1))
                vev.append(B.mark(ev))
                vi += 1
        exm = {}
        proj_mm_mark = B.mark(mm)
        proj_ev_mark = B.mark(ev)
        pv_mark = 0
        nr_mark = B.mark(nr)
        ev_mark = B.mark(ev)
        for hp in range(8):
            B.wait("g", mm, proj_mm_mark)
            for c in range(DC):
                B.dma("g", wqp[:, c * 128:(c + 1) * 128],
                      wq[c * 128:(c + 1) * 128, hp * 128:(hp + 1) * 128], wl)
                B.dma("g", wkp[:, c * 128:(c + 1) * 128],
                      wk[c * 128:(c + 1) * 128, hp * 128:(hp + 1) * 128], wl)
            bqc = bqc8[:, hp:hp + 1]
            bkc = bkc8[:, hp:hp + 1]
            wlv = B.mark(wl)
            for qt in range(NQT):
                ts = slice(qt * QT, (qt + 1) * QT)
                B.wait("t", wl, wlv)
                B.wait("t", ev, B.mark(ev))
                B.wait("t", nr, B.mark(nr))
                if modT is not None:
                    B.op("t", "matmul", PS.b[2][:], C.pairsel[hp][:], modT[seq][:, ts],
                         start=True, stop=True, inc=(mm, 1))
                for c in range(DC):
                    B.op("t", "matmul", PS.b[0][:], wqp[:, c * 128:(c + 1) * 128],
                         xn_sb[:, c * S + qt * QT:c * S + (qt + 1) * QT],
                         start=(c == 0), stop=(c == DC - 1),
                         inc=((mm, 1) if c == DC - 1 else None))
                for c in range(DC):
                    B.op("t", "matmul", PS.b[1][:], wkp[:, c * 128:(c + 1) * 128],
                         xn_sb[:, c * S + qt * QT:c * S + (qt + 1) * QT],
                         start=(c == 0), stop=(c == DC - 1),
                         inc=((mm, 1) if c == DC - 1 else None))
                mmv = B.mark(mm)
                if modT is not None:
                    B.wait("s", mm, mmv)
                    B.op("s", "copy", modbc[:], PS.b[2][:], inc=(nr, 1))
                    B.wait("v", nr, B.mark(nr))
                B.wait("v", mm, mmv)
                if rope:
                    emit_rope_pair(B, C, PS.b[0], bqc, qtp, ts, rst, ev)
                    emit_rope_pair(B, C, PS.b[1], bkc, ktp, ts, rst, ev)
                else:
                    if modT is not None:
                        B.op("v", "tensor_scalar_add", qtmp[:], PS.b[0][:], bqc)
                        B.op("v", "tensor_tensor", qtp[:, ts], qtmp[:], modbc[:], ALU.mult, inc=(ev, 1))
                    else:
                        B.op("v", "tensor_scalar_add", qtp[:, ts], PS.b[0][:], bqc, inc=(ev, 1))
                    B.op("v", "tensor_scalar_add", ktp[:, ts], PS.b[1][:], bkc, inc=(ev, 1))
            proj_mm_mark = B.mark(mm)
            proj_ev_mark = B.mark(ev)
            for qt in range(NQT):
                ts = slice(qt * QT, (qt + 1) * QT)
                for kc in range(KC):
                    buf = PS.A[kc % 2]
                    if kc >= 2:
                        B.wait("t", ex, exm[(hp, qt, kc - 2)])
                    elif qt > 0:
                        B.wait("t", ex, exm[(hp, qt - 1, KC - 2 + kc)])
                    elif hp > 0:
                        B.wait("t", ex, exm[(hp - 1, NQT - 1, KC - 2 + kc)])
                    if kc == 0:
                        B.wait("t", ev, B.mark(ev))
                    B.op("t", "matmul", buf[:, 0:512], ktp[0:64, kc * 128:(kc + 1) * 128],
                         qtp[0:64, ts], start=True, stop=True, tile_position=(0, 0), inc=(mm, 1))
                    B.op("t", "matmul", buf[:, 512:1024], ktp[64:128, kc * 128:(kc + 1) * 128],
                         qtp[64:128, ts], start=True, stop=True, tile_position=(64, 0), inc=(mm, 1))
                    B.wait("s", mm, B.mark(mm))
                    if kc == 0 and pv_mark > 0:
                        B.wait("s", pv, pv_mark)
                    B.op("s", "activation", probs[:, kc * 1024:(kc + 1) * 1024], buf[:],
                         AF.Exp, bias=C.negc[:, 0:1], scale=exp_scale, inc=(ex, 1))
                    exm[(hp, qt, kc)] = B.mark(ex)
                for kc in range(KC):
                    B.wait("t", ex, exm[(hp, qt, kc)])
                    if kc == 0:
                        B.wait("t", ev, ev_mark)
                        B.wait("t", nr, nr_mark)
                    for h in range(2):
                        hg = 2 * hp + h
                        B.op("t", "matmul", PS.b[0][h * 64:(h + 1) * 64, :],
                             vt_sb[:, kc * 1024 + hg * 64:kc * 1024 + (hg + 1) * 64],
                             probs[:, kc * 1024 + h * 512:kc * 1024 + (h + 1) * 512],
                             start=(kc == 0), stop=(kc == KC - 1),
                             tile_position=(0, h * 64), skip_group_check=True)
                        B.op("t", "matmul", PS.b[1][h * 64:h * 64 + 1, :],
                             C.ones_col_bf[:],
                             probs[:, kc * 1024 + h * 512:kc * 1024 + (h + 1) * 512],
                             start=(kc == 0), stop=(kc == KC - 1),
                             tile_position=(0, h * 64), skip_group_check=True,
                             inc=((pv, 1) if kc == KC - 1 and h == 1 else None))
                pv_mark = B.mark(pv)
                B.wait("v", pv, pv_mark)
                B.op("v", "reciprocal", rec[0][:], PS.b[1][0:1, :], lp=True)
                B.op("v", "reciprocal", rec[1][:], PS.b[1][64:65, :], inc=(nr, 1), lp=True)
                nr_mark = B.mark(nr)
                B.wait("t", nr, nr_mark)
                B.wait("t", nr if modT is None else nr, nr_mark)
                B.op("t", "matmul", PS.b[2][0:64, :], C.ones_row_bf[0:1, 0:64], rec[0][:],
                     start=True, stop=True, tile_position=(0, 0), inc=(mm, 1))
                B.op("t", "matmul", PS.b[2][64:128, :], C.ones_row_bf[0:1, 0:64], rec[1][:],
                     start=True, stop=True, tile_position=(0, 64), inc=(mm, 1))
                B.wait("s", mm, B.mark(mm))
                B.op("s", "copy", recbc[:], PS.b[2][:], inc=(nr, 1))
                nr_mark = B.mark(nr)
                B.wait("v", nr, nr_mark)
                B.op("v", "tensor_tensor", ctxo[:], PS.b[0][:], recbc[:], ALU.mult, inc=(ev, 1))
                ev_mark = B.mark(ev)
                B.wait("y", ev, ev_mark)
                B.dma("y", ctx_dst[seq, hp * 128:(hp + 1) * 128, ts], ctxo[:], sp)


def emit_rope_pair(B, C, psum, bias_col, out_pair, ts, rst, ev):
    st, st2, st3 = rst
    for h in range(2):
        hb = h * 64
        B.op("v", "tensor_scalar_add", st[:], psum[hb:hb + 64, :], bias_col[hb:hb + 64, :])
        B.op("v", "tensor_tensor", st3[:], st[:], C.rcos[:, ts], ALU.mult)
        B.op("v", "tensor_tensor", st2[0:32, :], st[32:64, :], C.rsin[32:64, ts], ALU.mult)
        B.op("v", "tensor_tensor", st2[32:64, :], st[0:32, :], C.rsin[0:32, ts], ALU.mult)
        B.op("v", "tensor_tensor", st3[:], st3[:], st2[:], ALU.add)
        B.op("v", "tensor_copy", out_pair[hb:hb + 64, ts], st3[:], inc=(ev, 1))


def emit_proj(B, C, PS, tag, src_h, dst_h, wT, bias_c, mode="bias",
              res_h=None, w_res=0.0, w_new=1.0, dst_dt=BF16):
    with ExitStack() as sc:
        wsb = B.sbuf(f"{tag}_w", [128, 8192], BF16, sc)
        bcol = B.sbuf(f"{tag}_b", [128, 8], F32, sc)
        xt = [[B.sbuf(f"{tag}_x{b}_{c}", [128, QT], BF16, sc) for c in range(DC)] for b in range(2)]
        rt = [B.sbuf(f"{tag}_r{i}", [128, QT], BF16, sc) for i in range(2)]
        ot = [B.sbuf(f"{tag}_o{i}", [128, QT], dst_dt, sc) for i in range(2)]
        tmp = B.sbuf(f"{tag}_tmp", [128, QT], F32, sc)
        ld, wl, mm, ev, sp = "ld", "wl", "mm", "ev", "sp"
        for kc in range(DC):
            B.dma("g", wsb[:, kc * 1024:(kc + 1) * 1024], wT[kc * 128:(kc + 1) * 128, :], wl)
        B.dma("y", bcol[:], bias_c, wl)
        wlv = B.mark(wl)
        sp_marks = []
        ev_marks = []
        xt_guard = [0, 0]
        it = 0
        for seq in range(SEQ):
            for qt in range(NQT):
                ts = slice(qt * QT, (qt + 1) * QT)
                xb = xt[it % 2]
                if xt_guard[it % 2]:
                    B.wait("y", mm, xt_guard[it % 2])
                for c in range(DC):
                    B.dma("y", xb[c][:], src_h[seq, c * 128:(c + 1) * 128, ts], ld)
                B.wait("t", ld, B.mark(ld))
                B.wait("t", wl, wlv)
                for m in range(DC):
                    bank = PS.b[m % 2]
                    bufi = m % 2
                    if mode != "bias":
                        if len(ev_marks) >= 2:
                            B.wait("y", ev, ev_marks[-2])
                        B.dma("y", rt[bufi][:], res_h[seq, m * 128:(m + 1) * 128, ts], ld)
                    if len(ev_marks) >= 2:
                        B.wait("t", ev, ev_marks[-2])
                    for kc in range(DC):
                        B.op("t", "matmul", bank[:], wsb[:, kc * 1024 + m * 128:kc * 1024 + (m + 1) * 128],
                             xb[kc][:], start=(kc == 0), stop=(kc == DC - 1),
                             inc=((mm, 1) if kc == DC - 1 else None))
                    B.wait("v", mm, B.mark(mm))
                    if mode != "bias":
                        B.wait("v", ld, B.mark(ld))
                    if len(sp_marks) >= 2:
                        B.wait("v", sp, sp_marks[-2])
                    if mode == "bias":
                        B.op("v", "tensor_scalar_add", ot[bufi][:], bank[:], bcol[:, m:m + 1], inc=(ev, 1))
                    elif mode == "mix":
                        B.op("v", "tensor_scalar", tmp[:], bank[:], bcol[:, m:m + 1], w_new,
                             ALU.add, ALU.mult)
                        B.op("v", "scalar_tensor_tensor", ot[bufi][:], rt[bufi][:], w_res, tmp[:],
                             ALU.mult, ALU.add, inc=(ev, 1))
                    elif mode == "bias_res":
                        B.op("v", "tensor_scalar_add", tmp[:], bank[:], bcol[:, m:m + 1])
                        B.op("v", "tensor_tensor", ot[bufi][:], tmp[:], rt[bufi][:], ALU.add, inc=(ev, 1))
                    ev_marks.append(B.mark(ev))
                    B.wait("y", ev, B.mark(ev))
                    B.dma("y", dst_h[seq, m * 128:(m + 1) * 128, ts], ot[bufi][:], sp)
                    sp_marks.append(B.mark(sp))
                xt_guard[it % 2] = B.mark(mm)
                it += 1


def emit_rms(B, C, PS, tag, src_h, dst_h, w_c):
    with ExitStack() as sc:
        xf = [B.sbuf(f"{tag}_xf{c}", [128, QT], BF16, sc) for c in range(DC)]
        sq = [B.sbuf(f"{tag}_sq{c}", [128, QT], BF16, sc) for c in range(DC)]
        wcol = B.sbuf(f"{tag}_wcol", [128, 8], F32, sc)
        r_msq = B.sbuf(f"{tag}_rmsq", [1, QT], F32, sc)
        r_std = B.sbuf(f"{tag}_rstd", [1, QT], F32, sc)
        r_rstd = B.sbuf(f"{tag}_rrstd", [1, QT], BF16, sc)
        xout = [B.sbuf(f"{tag}_xo{i}", [128, QT], BF16, sc) for i in range(2)]
        ld, sqd, mmd, rowd, bcd, evd, spl = "ld", "sq", "mm", "row", "bc", "ev", "sp"
        B.dma("y", wcol[:], w_c, ld)
        sp_marks = []
        it = 0
        for seq in range(SEQ):
            for qt in range(NQT):
                ts = slice(qt * QT, (qt + 1) * QT)
                if it > 0:
                    B.wait("y", evd, B.mark(evd))
                for c in range(DC):
                    B.dma("y", xf[c][:], src_h[seq, c * 128:(c + 1) * 128, ts], ld)
                ldv = B.mark(ld)
                B.wait("s", ld, ldv)
                for c in range(DC):
                    B.op("s", "activation", sq[c][:], xf[c][:], AF.Square, inc=(sqd, 1))
                B.wait("t", sqd, B.mark(sqd))
                if it > 0:
                    B.wait("t", rowd, B.mark(rowd))
                for c in range(DC):
                    B.op("t", "matmul", PS.A[0][0:1, 0:512], C.ones_col_bf[:], sq[c][:],
                         start=(c == 0), stop=(c == DC - 1),
                         inc=((mmd, 1) if c == DC - 1 else None))
                B.wait("v", mmd, B.mark(mmd))
                B.op("v", "tensor_scalar_mul", r_msq[:], PS.A[0][0:1, 0:512], float(1.0 / D), inc=(rowd, 1))
                B.wait("s", rowd, B.mark(rowd))
                B.op("s", "activation", r_std[:], r_msq[:], AF.Sqrt, bias=C.eps_rms[:, 0:1], inc=(rowd, 1))
                B.wait("v", rowd, B.mark(rowd))
                B.op("v", "reciprocal", r_rstd[:], r_std[:], inc=(rowd, 1), lp=True)
                B.wait("t", rowd, B.mark(rowd))
                if it > 0:
                    B.wait("t", evd, B.mark(evd))
                B.op("t", "matmul", PS.b[0][:], C.ones_row_bf[0:1, :], r_rstd[:], start=True, stop=True, inc=(bcd, 1))
                B.wait("v", bcd, B.mark(bcd))
                for c in range(DC):
                    buf = c % 2
                    if len(sp_marks) >= 2:
                        B.wait("v", spl, sp_marks[-2])
                    B.op("v", "scalar_tensor_tensor", xout[buf][:], xf[c][:], wcol[:, c:c + 1],
                         PS.b[0][:], ALU.mult, ALU.mult, inc=(evd, 1))
                    B.wait("y", evd, B.mark(evd))
                    B.dma("y", dst_h[seq, c * 128:(c + 1) * 128, ts], xout[buf][:], spl)
                    sp_marks.append(B.mark(spl))
                it += 1


def emit_attn_hd256(B, C, PS, tag, seq, src_h, win, bin1d, bin_c, ctx_dst):
    with ExitStack() as sc:
        xn_sb = B.sbuf(f"{tag}_xn", [128, 16384], BF16, sc)
        vt_sb = B.sbuf(f"{tag}_vt", [128, 16384], BF16, sc)
        wsb = B.sbuf(f"{tag}_w", [128, 8192], BF16, sc)
        wk_sb = B.sbuf(f"{tag}_wk", [128, 2048], BF16, sc)
        wq_sb = B.sbuf(f"{tag}_wq", [128, 2048], BF16, sc)
        qth = B.sbuf(f"{tag}_qth", [128, 2 * S], BF16, sc)
        kth = B.sbuf(f"{tag}_kth", [128, 2 * S], BF16, sc)
        probs = B.sbuf(f"{tag}_pr", [128, 16384], BF16, sc)
        bvbc = B.sbuf(f"{tag}_bvbc", [128, 1024], F32, sc)
        bvrow = B.sbuf(f"{tag}_bvrow", [1, 1024], BF16, sc)
        binc = B.sbuf(f"{tag}_binc", [128, 24], F32, sc)
        rec0 = B.sbuf(f"{tag}_rec", [1, QT], BF16, sc)
        recbc = B.sbuf(f"{tag}_recbc", [128, QT], F32, sc)
        ctxo = [B.sbuf(f"{tag}_ctxo{i}", [128, QT], BF16, sc) for i in range(2)]
        ld, wl, mm, ex, pv, nr, ev, sp = "ld", "wl", "mm", "ex", "pv", "nr", "ev", "sp"
        for c in range(DC):
            for qq in range(NQT):
                B.dma("y", xn_sb[:, c * S + qq * QT:c * S + (qq + 1) * QT],
                      src_h[seq, c * 128:(c + 1) * 128, qq * QT:(qq + 1) * QT], ld)
        for c in range(DC):
            B.dma("g", wsb[:, c * 1024:(c + 1) * 1024],
                  win[c * 128:(c + 1) * 128, 2 * D:3 * D], wl)
        B.dma("g", bvrow[:], row_f32(bin1d, 2 * D, 3 * D), wl)
        B.dma("y", binc[:], bin_c, wl)
        B.wait("t", wl, B.mark(wl))
        B.op("t", "matmul", PS.b[0][:], C.ones_row_bf[:], bvrow[:, 0:512], start=True, stop=True, inc=(mm, 1))
        B.op("t", "matmul", PS.b[1][:], C.ones_row_bf[:], bvrow[:, 512:1024], start=True, stop=True, inc=(mm, 1))
        B.wait("s", mm, B.mark(mm))
        B.op("s", "copy", bvbc[:, 0:512], PS.b[0][:], inc=(nr, 1))
        B.op("s", "copy", bvbc[:, 512:1024], PS.b[1][:], inc=(nr, 1))
        B.wait("t", ld, B.mark(ld))
        B.wait("t", nr, B.mark(nr))
        vev = []
        vi = 0
        for tt in range(16):
            for grp in range(2):
                bank = PS.b[vi % 2]
                if vi >= 2:
                    B.wait("t", ev, vev[vi - 2])
                for c in range(DC):
                    B.op("t", "matmul", bank[:],
                         xn_sb[:, c * S + tt * 128:c * S + (tt + 1) * 128],
                         wsb[:, c * 1024 + grp * 512:c * 1024 + (grp + 1) * 512],
                         start=(c == 0), stop=(c == DC - 1),
                         inc=((mm, 1) if c == DC - 1 else None))
                B.wait("v", mm, B.mark(mm))
                B.op("v", "tensor_tensor",
                     vt_sb[:, tt * 1024 + grp * 512:tt * 1024 + (grp + 1) * 512],
                     bank[:], bvbc[:, grp * 512:(grp + 1) * 512], ALU.add, inc=(ev, 1))
                vev.append(B.mark(ev))
                vi += 1
        exm = {}
        proj_mm_mark = B.mark(mm)
        proj_ev_mark = B.mark(ev)
        pv_mark = 0
        nr_mark = B.mark(nr)
        ev_mark = B.mark(ev)
        for hd in range(4):
            B.wait("g", mm, proj_mm_mark)
            for c in range(DC):
                B.dma("g", wq_sb[:, c * 256:(c + 1) * 256],
                      win[c * 128:(c + 1) * 128, hd * 256:(hd + 1) * 256], wl)
                B.dma("g", wk_sb[:, c * 256:(c + 1) * 256],
                      win[c * 128:(c + 1) * 128, D + hd * 256:D + (hd + 1) * 256], wl)
            wlv = B.mark(wl)
            for qt in range(NQT):
                ts = slice(qt * QT, (qt + 1) * QT)
                B.wait("t", wl, wlv)
                for cc in range(2):
                    B.wait("t", ev, B.mark(ev))
                    for c in range(DC):
                        B.op("t", "matmul", PS.b[0][:], wq_sb[:, c * 256 + cc * 128:c * 256 + (cc + 1) * 128],
                             xn_sb[:, c * S + qt * QT:c * S + (qt + 1) * QT],
                             start=(c == 0), stop=(c == DC - 1),
                             inc=((mm, 1) if c == DC - 1 else None))
                    for c in range(DC):
                        B.op("t", "matmul", PS.b[1][:], wk_sb[:, c * 256 + cc * 128:c * 256 + (cc + 1) * 128],
                             xn_sb[:, c * S + qt * QT:c * S + (qt + 1) * QT],
                             start=(c == 0), stop=(c == DC - 1),
                             inc=((mm, 1) if c == DC - 1 else None))
                    B.wait("v", mm, B.mark(mm))
                    B.op("v", "tensor_scalar_add", qth[:, cc * S + qt * QT:cc * S + (qt + 1) * QT],
                         PS.b[0][:], binc[:, hd * 2 + cc:hd * 2 + cc + 1], inc=(ev, 1))
                    B.op("v", "tensor_scalar_add", kth[:, cc * S + qt * QT:cc * S + (qt + 1) * QT],
                         PS.b[1][:], binc[:, 8 + hd * 2 + cc:8 + hd * 2 + cc + 1], inc=(ev, 1))
            proj_mm_mark = B.mark(mm)
            proj_ev_mark = B.mark(ev)
            for qt in range(NQT):
                ts = slice(qt * QT, (qt + 1) * QT)
                for kc in range(KC):
                    buf = PS.A[(kc // 2) % 2]
                    half = kc % 2
                    if kc >= 4:
                        B.wait("t", ex, exm[(hd, qt, kc - 4)])
                    elif qt > 0:
                        B.wait("t", ex, exm[(hd, qt - 1, KC - 4 + kc)])
                    elif hd > 0:
                        B.wait("t", ex, exm[(hd - 1, NQT - 1, KC - 4 + kc)])
                    if kc == 0:
                        B.wait("t", ev, B.mark(ev))
                        B.wait("t", nr, nr_mark)
                    for cc in range(2):
                        B.op("t", "matmul", buf[:, half * 512:(half + 1) * 512],
                             kth[:, cc * S + kc * 128:cc * S + (kc + 1) * 128],
                             qth[:, cc * S + qt * QT:cc * S + (qt + 1) * QT],
                             start=(cc == 0), stop=(cc == 1),
                             inc=((mm, 1) if cc == 1 else None))
                    if half == 1:
                        B.wait("s", mm, B.mark(mm))
                        if kc == 1 and pv_mark > 0:
                            B.wait("s", pv, pv_mark)
                        B.op("s", "activation", probs[:, (kc - 1) * QT:(kc + 1) * QT], buf[:],
                             AF.Exp, bias=C.negc[:, 0:1], scale=0.0625, inc=(ex, 1))
                        exm[(hd, qt, kc - 1)] = exm[(hd, qt, kc)] = B.mark(ex)
                for kc in range(KC):
                    B.wait("t", ex, exm[(hd, qt, kc)])
                    if kc == 0:
                        B.wait("t", ev, ev_mark)
                        B.wait("t", nr, nr_mark)
                    for cc in range(2):
                        B.op("t", "matmul", PS.b[cc][:],
                             vt_sb[:, kc * 1024 + hd * 256 + cc * 128:kc * 1024 + hd * 256 + (cc + 1) * 128],
                             probs[:, kc * QT:(kc + 1) * QT],
                             start=(kc == 0), stop=(kc == KC - 1), skip_group_check=True)
                    B.op("t", "matmul", PS.b[2][0:1, :], C.ones_col_bf[:],
                         probs[:, kc * QT:(kc + 1) * QT],
                         start=(kc == 0), stop=(kc == KC - 1), skip_group_check=True,
                         inc=((pv, 1) if kc == KC - 1 else None))
                pv_mark = B.mark(pv)
                B.wait("v", pv, pv_mark)
                B.op("v", "reciprocal", rec0[:], PS.b[2][0:1, :], lp=True, inc=(nr, 1))
                nr_mark = B.mark(nr)
                B.wait("t", nr, nr_mark)
                B.op("t", "matmul", PS.A[0][:, 0:512], C.ones_row_bf[:], rec0[:],
                     start=True, stop=True, inc=(mm, 1))
                B.wait("s", mm, B.mark(mm))
                B.op("s", "copy", recbc[:], PS.A[0][:, 0:512], inc=(nr, 1))
                nr_mark = B.mark(nr)
                B.wait("v", nr, nr_mark)
                B.op("v", "tensor_tensor", ctxo[0][:], PS.b[0][:], recbc[:], ALU.mult, inc=(ev, 1))
                B.op("v", "tensor_tensor", ctxo[1][:], PS.b[1][:], recbc[:], ALU.mult, inc=(ev, 1))
                ev_mark = B.mark(ev)
                B.wait("y", ev, ev_mark)
                B.dma("y", ctx_dst[seq, hd * 256:hd * 256 + 128, ts], ctxo[0][:], sp)
                B.dma("y", ctx_dst[seq, hd * 256 + 128:hd * 256 + 256, ts], ctxo[1][:], sp)


def emit_comb(B, C, PS, H, combT_sb):
    with ExitStack() as sc:
        gw = B.sbuf("cb_gw", [128, 64], BF16, sc)
        yt = [B.sbuf(f"cb_y{c}", [128, QT], BF16, sc) for c in range(DC)]
        m1 = B.sbuf("cb_m1", [128, 1], F32, sc)
        m2 = B.sbuf("cb_m2", [128, 1], F32, sc)
        msk1 = B.sbuf("cb_msk1", [128, 8], F32, sc)
        msk2 = B.sbuf("cb_msk2", [128, 8], F32, sc)
        l2 = B.sbuf("cb_l2", [128, 8], F32, sc)
        dcol = B.sbuf("cb_dcol", [128, 1], F32, sc)
        s1c = B.sbuf("cb_s1c", [128, 1], F32, sc)
        s2c = B.sbuf("cb_s2c", [128, 1], F32, sc)
        comb = B.sbuf("cb_comb", [128, 8], F32, sc)
        combb = B.sbuf("cb_combb", [128, 8], BF16, sc)
        for c in range(DC):
            B.dma("g", gw[:, c * 8:(c + 1) * 8], H["l0f_gatewT"][c * 128:(c + 1) * 128, :], "wl")
        wlv = B.mark("wl")
        for seq in range(SEQ):
            for qt in range(NQT):
                ts = slice(qt * QT, (qt + 1) * QT)
                if seq + qt > 0:
                    B.wait("y", "mm", B.mark("mm"))
                for c in range(DC):
                    B.dma("y", yt[c][:], H["y_h"][seq, c * 128:(c + 1) * 128, ts], "ld")
                ldv = B.mark("ld")
                B.wait("t", "ld", ldv)
                B.wait("t", "wl", wlv)
                for tt in range(4):
                    bank = PS.b[tt % 2]
                    B.wait("t", "ev", B.mark("ev"))
                    for c in range(DC):
                        B.op("t", "matmul", bank[0:128, 0:8],
                             yt[c][:, tt * 128:(tt + 1) * 128], gw[:, c * 8:(c + 1) * 8],
                             start=(c == 0), stop=(c == DC - 1),
                             inc=(("mm", 1) if c == DC - 1 else None))
                    B.wait("v", "mm", B.mark("mm"))
                    B.op("v", "reduce_max", m1[:], bank[0:128, 0:8], axis=AX.X, inc=("nr", 1))
                    B.wait("v", "nr", B.mark("nr"))
                    B.op("v", "tensor_scalar", msk1[:], bank[0:128, 0:8], m1[:, 0:1], None, ALU.is_ge, inc=("nr", 1))
                    B.wait("v", "nr", B.mark("nr"))
                    B.op("v", "scalar_tensor_tensor", l2[:], msk1[:], -1e30, bank[0:128, 0:8],
                         ALU.mult, ALU.add, inc=("nr", 1))
                    B.wait("v", "nr", B.mark("nr"))
                    B.op("v", "reduce_max", m2[:], l2[:], axis=AX.X, inc=("nr", 1))
                    B.wait("v", "nr", B.mark("nr"))
                    B.op("v", "tensor_scalar", msk2[:], l2[:], m2[:, 0:1], None, ALU.is_ge)
                    B.op("v", "tensor_tensor", dcol[:], m2[:], m1[:], ALU.subtract, inc=("nr", 1))
                    B.wait("s", "nr", B.mark("nr"))
                    B.op("s", "activation", s2c[:], dcol[:], AF.Sigmoid)
                    B.op("s", "activation", s1c[:], dcol[:], AF.Sigmoid, scale=-1.0, inc=("nr", 1))
                    B.wait("v", "nr", B.mark("nr"))
                    B.op("v", "tensor_scalar_mul", comb[:], msk2[:], s2c[:, 0:1], inc=("nr", 1))
                    B.wait("v", "nr", B.mark("nr"))
                    B.op("v", "scalar_tensor_tensor", comb[:], msk1[:], s1c[:, 0:1], comb[:],
                         ALU.mult, ALU.add, inc=("nr", 1))
                    B.wait("v", "nr", B.mark("nr"))
                    B.op("v", "tensor_copy", combb[:], comb[:], inc=("ev", 1))
                    B.wait("t", "ev", B.mark("ev"))
                    B.wait("t", "nr", B.mark("nr"))
                    B.op("t", "matmul", PS.b[2][0:8, 0:128], combb[:], C.ident[:],
                         start=True, stop=True, inc=("mm", 1))
                    B.wait("v", "mm", B.mark("mm"))
                    B.op("v", "tensor_copy", combT_sb[seq][:, qt * QT + tt * 128:qt * QT + (tt + 1) * 128],
                         PS.b[2][0:8, 0:128], inc=("nr", 1))
                    if DEV:
                        B.wait("g", "nr", B.mark("nr"))
                        B.dma("g", H["combT_h"][seq, :, qt * QT + tt * 128:qt * QT + (tt + 1) * 128],
                              combT_sb[seq][:, qt * QT + tt * 128:qt * QT + (tt + 1) * 128], "sp")


def emit_moe(B, C, PS, H, combT_sb):
    with ExitStack() as sc:
        w1s = B.sbuf("mo_w1", [128, 16384], BF16, sc)
        w2s = B.sbuf("mo_w2", [128, 16384], BF16, sc)
        w3s = B.sbuf("mo_w3", [128, 16384], BF16, sc)
        acc = B.sbuf("mo_acc", [128, 16384], BF16, sc)
        h_sb = B.sbuf("mo_h", [128, 8192], BF16, sc)
        yt = [[B.sbuf(f"mo_y{b}_{c}", [128, QT], BF16, sc) for c in range(DC)] for b in range(2)]
        silu_sb = [B.sbuf(f"mo_sl{i}", [128, QT], F32, sc) for i in range(2)]
        combbc = B.sbuf("mo_cbc", [128, QT], F32, sc)
        b1c = B.sbuf("mo_b1c", [128, 16], F32, sc)
        b2c = B.sbuf("mo_b2c", [128, 16], F32, sc)
        b3c = B.sbuf("mo_b3c", [128, 8], F32, sc)
        tmp = B.sbuf("mo_tmp", [128, QT], F32, sc)
        tx1 = [B.sbuf(f"mo_tx1{i}", [128, QT], BF16, sc) for i in range(2)]
        tss = [B.sbuf(f"mo_ts{i}", [128, QT], BF16, sc) for i in range(2)]
        txx = B.sbuf("mo_tx", [128, QT], F32, sc)
        tg = B.sbuf("mo_tg", [128, QT], BF16, sc)
        tout = B.sbuf("mo_to", [128, QT], F32, sc)

        def load_w(e, which, slot, sem="wl"):
            src = H[f"l0f_w{which}T"]
            n = 2048 if which in (1, 2) else 1024
            kc_n = 8 if which in (1, 2) else 16
            for kc in range(kc_n):
                B.dma("g", slot[:, kc * n:(kc + 1) * n], src[e, kc * 128:(kc + 1) * 128, :], sem)

        def load_bias_cols(e):
            B.dma("y", b1c[:], H["l0f_b1_c"][e], "wl")
            B.dma("y", b2c[:], H["l0f_b2_c"][e], "wl")
            B.dma("y", b3c[:], H["l0f_b3_c"][e], "wl")

        ybuf_guard = [0, 0]
        load_w(0, 1, w1s)
        load_w(0, 2, w2s)
        load_w(0, 3, w3s)
        tail_sp = []
        tail_ev = []
        tp2 = tmp
        for seq in range(SEQ):
            for e in range(8):
                B.wait("y", "ev", B.mark("ev"))
                B.wait("y", "nr", B.mark("nr"))
                load_bias_cols(e)
                wlv = B.mark("wl")
                h_marks = []
                for qt in range(NQT):
                    ts = slice(qt * QT, (qt + 1) * QT)
                    ybuf = yt[qt % 2]
                    if ybuf_guard[qt % 2]:
                        B.wait("y", "mm", ybuf_guard[qt % 2])
                    for c in range(DC):
                        B.dma("y", ybuf[c][:], H["y_h"][seq, c * 128:(c + 1) * 128, ts], "ld")
                    ldv = B.mark("ld")
                    B.wait("t", "nr", B.mark("nr"))
                    B.op("t", "matmul", PS.b[2][:], C.esel[e][:], combT_sb[seq][:, ts],
                         start=True, stop=True, inc=("mm", 1))
                    B.wait("s", "mm", B.mark("mm"))
                    B.op("s", "copy", combbc[:], PS.b[2][:], inc=("nr", 1))
                    B.wait("t", "ld", ldv)
                    B.wait("t", "wl", wlv)
                    sst_marks = []
                    for hc in range(16):
                        if hc >= 1:
                            B.wait("t", "nr", B.mark("nr"))
                        for kc in range(DC):
                            B.op("t", "matmul", PS.b[0][:],
                                 w1s[:, kc * 2048 + hc * 128:kc * 2048 + (hc + 1) * 128],
                                 ybuf[kc][:], start=(kc == 0), stop=(kc == DC - 1),
                                 inc=(("mm", 1) if kc == DC - 1 else None))
                        B.wait("s", "mm", B.mark("mm"))
                        if hc >= 2:
                            B.wait("s", "ev", sst_marks[hc - 2])
                        B.op("s", "activation", silu_sb[hc % 2][:], PS.b[0][:], AF.Silu,
                             bias=b1c[:, hc:hc + 1], inc=("nr", 1))
                        snr = B.mark("nr")
                        if hc >= 1:
                            B.wait("t", "ev", B.mark("ev"))
                        for kc in range(DC):
                            B.op("t", "matmul", PS.b[1][:],
                                 w2s[:, kc * 2048 + hc * 128:kc * 2048 + (hc + 1) * 128],
                                 ybuf[kc][:], start=(kc == 0), stop=(kc == DC - 1),
                                 inc=(("mm", 1) if kc == DC - 1 else None))
                        B.wait("v", "mm", B.mark("mm"))
                        B.wait("v", "nr", snr)
                        B.op("v", "scalar_tensor_tensor", h_sb[:, hc * QT:(hc + 1) * QT],
                             PS.b[1][:], b2c[:, hc:hc + 1], silu_sb[hc % 2][:],
                             ALU.add, ALU.mult, inc=("ev", 1))
                        sst_marks.append(B.mark("ev"))
                    hdone = B.mark("ev")
                    h_marks.append(B.mark("mm"))
                    ybuf_guard[qt % 2] = B.mark("mm")
                    yev_marks = []
                    for m in range(DC):
                        bank = PS.b[m % 2]
                        B.wait("t", "ev", hdone if m < 2 else yev_marks[m - 2])
                        for kc in range(16):
                            B.op("t", "matmul", bank[:],
                                 w3s[:, kc * 1024 + m * 128:kc * 1024 + (m + 1) * 128],
                                 h_sb[:, kc * QT:(kc + 1) * QT],
                                 start=(kc == 0), stop=(kc == 15),
                                 inc=(("mm", 1) if kc == 15 else None))
                        B.wait("v", "mm", B.mark("mm"))
                        aslice = acc[:, m * 2048 + qt * QT:m * 2048 + (qt + 1) * QT]
                        if e == 0:
                            B.op("v", "scalar_tensor_tensor", aslice, bank[:],
                                 b3c[:, m:m + 1], combbc[:], ALU.add, ALU.mult, inc=("ev", 1))
                        else:
                            B.op("v", "scalar_tensor_tensor", tmp[:], bank[:],
                                 b3c[:, m:m + 1], combbc[:], ALU.add, ALU.mult, inc=("ev", 1))
                            B.wait("v", "ev", B.mark("ev"))
                            B.op("v", "tensor_tensor", aslice, aslice, tmp[:], ALU.add, inc=("ev", 1))
                        yev_marks.append(B.mark("ev"))
                        if e == 7:
                            c = m
                            bufi = c % 2
                            if len(tail_sp) >= 2:
                                B.wait("y", "sp", tail_sp[-2])
                            B.dma("y", tx1[bufi][:], H["x1_h"][seq, c * 128:(c + 1) * 128, ts], "ld")
                            B.dma("y", tss[bufi][:], H["s_h"][seq, c * 128:(c + 1) * 128, ts], "ld")
                            if tail_ev:
                                B.wait("y", "ev", tail_ev[-1])
                            B.dma("y", txx[:], H["xT"][seq, c * 128:(c + 1) * 128, ts], "ld")
                            ldv2 = B.mark("ld")
                            B.wait("v", "ld", ldv2)
                            B.op("v", "tensor_tensor", tp2[:], tx1[bufi][:], aslice, ALU.add, inc=("nr", 1))
                            B.wait("s", "nr", B.mark("nr"))
                            B.op("s", "activation", tg[:], tp2[:], AF.Silu, inc=("nr", 1))
                            B.wait("v", "nr", B.mark("nr"))
                            B.op("v", "tensor_tensor", tp2[:], tss[bufi][:], tg[:], ALU.mult)
                            if tail_sp:
                                B.wait("v", "sp", tail_sp[-1])
                            B.op("v", "tensor_tensor", tout[:], txx[:], tp2[:], ALU.add, inc=("ev", 1))
                            tail_ev.append(B.mark("ev"))
                            B.wait("y", "ev", B.mark("ev"))
                            B.dma("y", H["xr_h"][seq, c * 128:(c + 1) * 128, ts], tout[:], "sp")
                            tail_sp.append(B.mark("sp"))
                    if qt == NQT - 1:
                        if not (seq == SEQ - 1 and e == 7):
                            ne = (e + 1) % 8
                            B.wait("g", "mm", h_marks[-1])
                            load_w(ne, 1, w1s)
                            load_w(ne, 2, w2s)
                            B.wait("g", "mm", B.mark("mm"))
                            load_w(ne, 3, w3s)



def emit_ffn1(B, C, PS, H):
    HC1 = 24
    with ExitStack() as sc:
        wg_sb = B.sbuf("f1_wg", [128, 24576], BF16, sc)
        wu_sb = B.sbuf("f1_wu", [128, 24576], BF16, sc)
        wd_sb = B.sbuf("f1_wd", [128, 24576], BF16, sc)
        h_sb = B.sbuf("f1_h", [128, HC1 * QT], BF16, sc)
        xt = [[B.sbuf(f"f1_x{b}_{c}", [128, QT], BF16, sc) for c in range(DC)] for b in range(2)]
        silu_sb = [B.sbuf(f"f1_sl{i}", [128, QT], BF16, sc) for i in range(2)]
        tg = B.sbuf("f1_tg", [128, QT], BF16, sc)
        tp = B.sbuf("f1_tp", [128, QT], F32, sc)
        tss = B.sbuf("f1_ts", [128, QT], BF16, sc)
        txx = B.sbuf("f1_txx", [128, QT], F32, sc)
        tout = B.sbuf("f1_to", [128, QT], F32, sc)
        for kc in range(DC):
            B.dma("g", wg_sb[:, kc * 3072:(kc + 1) * 3072], H["l1f_wgT"][kc * 128:(kc + 1) * 128, :], "wl")
            B.dma("g", wu_sb[:, kc * 3072:(kc + 1) * 3072], H["l1f_wuT"][kc * 128:(kc + 1) * 128, :], "wl")
        for kc in range(HC1):
            B.dma("g", wd_sb[:, kc * 1024:(kc + 1) * 1024], H["l1f_wdT"][kc * 128:(kc + 1) * 128, :], "wl")
        wlv = B.mark("wl")
        sp_marks = []
        xt_guard = [0, 0]
        it = 0
        for seq in range(SEQ):
            for qt in range(NQT):
                ts = slice(qt * QT, (qt + 1) * QT)
                xc = xt[it % 2]
                if xt_guard[it % 2]:
                    B.wait("y", "mm", xt_guard[it % 2])
                for c in range(DC):
                    B.dma("y", xc[c][:], H["xn_h"][seq, c * 128:(c + 1) * 128, ts], "ld")
                ldv = B.mark("ld")
                B.wait("t", "ld", ldv)
                B.wait("t", "wl", wlv)
                sst_marks = []
                for hc in range(HC1):
                    if hc >= 1:
                        B.wait("t", "nr", B.mark("nr"))
                    for kc in range(DC):
                        B.op("t", "matmul", PS.b[0][:],
                             wg_sb[:, kc * 3072 + hc * 128:kc * 3072 + (hc + 1) * 128],
                             xc[kc][:], start=(kc == 0), stop=(kc == DC - 1),
                             inc=(("mm", 1) if kc == DC - 1 else None))
                    B.wait("s", "mm", B.mark("mm"))
                    if hc >= 2:
                        B.wait("s", "ev", sst_marks[hc - 2])
                    B.op("s", "activation", silu_sb[hc % 2][:], PS.b[0][:], AF.Silu,
                         scale=float(1.0 / 127.0), inc=("nr", 1))
                    snr = B.mark("nr")
                    if hc >= 1:
                        B.wait("t", "ev", B.mark("ev"))
                    for kc in range(DC):
                        B.op("t", "matmul", PS.b[1][:],
                             wu_sb[:, kc * 3072 + hc * 128:kc * 3072 + (hc + 1) * 128],
                             xc[kc][:], start=(kc == 0), stop=(kc == DC - 1),
                             inc=(("mm", 1) if kc == DC - 1 else None))
                    B.wait("v", "mm", B.mark("mm"))
                    B.wait("v", "nr", snr)
                    B.op("v", "scalar_tensor_tensor", h_sb[:, hc * QT:(hc + 1) * QT],
                         PS.b[1][:], float(1.0 / 127.0), silu_sb[hc % 2][:],
                         ALU.mult, ALU.mult, inc=("ev", 1))
                    sst_marks.append(B.mark("ev"))
                hdone = B.mark("ev")
                xt_guard[it % 2] = B.mark("mm")
                it += 1
                fev_marks = []
                for m in range(DC):
                    bank = PS.b[m % 2]
                    B.wait("t", "ev", hdone if m < 2 else fev_marks[m - 2])
                    for kc in range(HC1):
                        B.op("t", "matmul", bank[:],
                             wd_sb[:, kc * 1024 + m * 128:kc * 1024 + (m + 1) * 128],
                             h_sb[:, kc * QT:(kc + 1) * QT],
                             start=(kc == 0), stop=(kc == HC1 - 1),
                             inc=(("mm", 1) if kc == HC1 - 1 else None))
                    B.wait("y", "ev", B.mark("ev"))
                    B.dma("y", tss[:], H["s_h"][seq, m * 128:(m + 1) * 128, ts], "ld")
                    B.dma("y", txx[:], H["xr_h"][seq, m * 128:(m + 1) * 128, ts], "ld")
                    B.wait("s", "mm", B.mark("mm"))
                    B.wait("s", "ev", B.mark("ev"))
                    B.op("s", "activation", tg[:], bank[:], AF.Silu,
                         scale=float(1.0 / 127.0), inc=("nr", 1))
                    B.wait("v", "nr", B.mark("nr"))
                    B.wait("v", "ld", B.mark("ld"))
                    B.op("v", "tensor_tensor", tp[:], tss[:], tg[:], ALU.mult)
                    if sp_marks:
                        B.wait("v", "sp", sp_marks[-1])
                    B.op("v", "tensor_tensor", tout[:], txx[:], tp[:], ALU.add, inc=("ev", 1))
                    fev_marks.append(B.mark("ev"))
                    B.wait("y", "ev", B.mark("ev"))
                    B.dma("y", H["outT"][seq, m * 128:(m + 1) * 128, ts], tout[:], "sp")
                    sp_marks.append(B.mark("sp"))


def host_layout(inputs):
    t = lambda a: np.ascontiguousarray(a)
    shared = {
        "l0a_wq": t(inputs["L0_attn_Wq"]), "l0a_bq": t(inputs["L0_attn_bq"]),
        "l0a_wk": t(inputs["L0_attn_Wk"]), "l0a_bk": t(inputs["L0_attn_bk"]),
        "l0a_wv": t(inputs["L0_attn_Wv"]), "l0a_bv": t(inputs["L0_attn_bv"]),
        "l0a_cgw": t(inputs["L0_attn_cg_W"]), "l0a_cgb": t(inputs["L0_attn_cg_b"]),
        "l0a_amw": t(inputs["L0_attn_am_W"]), "l0a_amb": t(inputs["L0_attn_am_b"]),
        "l0a_cons": t(inputs["L0_attn_cons_vec"]),
        "l0a_cawin": t(inputs["L0_attn_ca_Win"]), "l0a_cabin": t(inputs["L0_attn_ca_bin"]),
        "l0a_cawout": t(inputs["L0_attn_ca_Wout"]), "l0a_cabout": t(inputs["L0_attn_ca_bout"]),
        "l0a_mcwin": t(inputs["L0_attn_mc_Win"]), "l0a_mcbin": t(inputs["L0_attn_mc_bin"]),
        "l0a_mcwout": t(inputs["L0_attn_mc_Wout"]), "l0a_mcbout": t(inputs["L0_attn_mc_bout"]),
        "l0a_wo": t(inputs["L0_attn_Wo"]), "l0a_bo": t(inputs["L0_attn_bo"]),
        "l0f_n1w": t(inputs["L0_ffn_norm1_w"]), "l0f_n2w": t(inputs["L0_ffn_norm2_w"]),
        "l0f_qkvwT": t(inputs["L0_ffn_qkv_w"].T), "l0f_qkvb": t(inputs["L0_ffn_qkv_b"]),
        "l0f_outwT": t(inputs["L0_ffn_out_w"].T), "l0f_outb": t(inputs["L0_ffn_out_b"]),
        "l0f_gatewT": t(inputs["L0_ffn_gate_w"].T),
        "l0f_w1T": t(inputs["L0_ffn_w1"].transpose(0, 2, 1)), "l0f_b1": t(inputs["L0_ffn_b1"]),
        "l0f_w2T": t(inputs["L0_ffn_w2"].transpose(0, 2, 1)), "l0f_b2": t(inputs["L0_ffn_b2"]),
        "l0f_w3T": t(inputs["L0_ffn_w3"].transpose(0, 2, 1)), "l0f_b3": t(inputs["L0_ffn_b3"]),
        "l1a_wqT": t(inputs["L1_attn_Wq"].T), "l1a_bq": t(inputs["L1_attn_bq"]),
        "l1a_wkT": t(inputs["L1_attn_Wk"].T), "l1a_bk": t(inputs["L1_attn_bk"]),
        "l1a_wvT": t(inputs["L1_attn_Wv"].T), "l1a_bv": t(inputs["L1_attn_bv"]),
        "l1a_woT": t(inputs["L1_attn_Wo"].T), "l1a_bo": t(inputs["L1_attn_bo"]),
        "l1f_wgT": t(inputs["L1_ffn_w_gate"].T), "l1f_wuT": t(inputs["L1_ffn_w_up"].T),
        "l1f_wdT": t(inputs["L1_ffn_w_down"].T),
    }
    cols = lambda a: np.ascontiguousarray(np.asarray(a).reshape(-1, 128).T)
    ecols = lambda a: np.ascontiguousarray(np.stack([cols(x) for x in a]))
    shared.update({
        "l0a_bq_c": cols(inputs["L0_attn_bq"]), "l0a_bk_c": cols(inputs["L0_attn_bk"]),
        "l0a_cabin_c": cols(inputs["L0_attn_ca_bin"]), "l0a_mcbin_c": cols(inputs["L0_attn_mc_bin"]),
        "l0a_cabout_c": cols(inputs["L0_attn_ca_bout"]), "l0a_mcbout_c": cols(inputs["L0_attn_mc_bout"]),
        "l0a_bo_c": cols(inputs["L0_attn_bo"]),
        "l0f_qkvb_c": cols(inputs["L0_ffn_qkv_b"]), "l0f_outb_c": cols(inputs["L0_ffn_out_b"]),
        "l0f_b1_c": ecols(inputs["L0_ffn_b1"]), "l0f_b2_c": ecols(inputs["L0_ffn_b2"]),
        "l0f_b3_c": ecols(inputs["L0_ffn_b3"]),
        "l0f_n1w_c": cols(inputs["L0_ffn_norm1_w"]), "l0f_n2w_c": cols(inputs["L0_ffn_norm2_w"]),
        "l1a_bq_c": cols(inputs["L1_attn_bq"]), "l1a_bk_c": cols(inputs["L1_attn_bk"]),
        "l1a_bo_c": cols(inputs["L1_attn_bo"]),
    })
    x = inputs["L0_attn_hidden_states"]
    in_maps = []
    for c in range(NCORE):
        m = dict(shared)
        m["xT"] = t(x[c * SEQ:(c + 1) * SEQ].transpose(0, 2, 1))
        in_maps.append(m)
    return in_maps


class PSums:
    pass


def build_program(phases):
    nc = bass.Bass("TRN2", target_bir_lowering=False)
    B = Builder(nc)
    H = declare_hbm(nc)
    PS = PSums()
    PS.A = [B.psum(f"psA{i}", [128, 1024], F32) for i in range(2)]
    PS.b = [B.psum(f"pb{i}", [128, 512], F32) for i in range(3)]
    PS.t = B.psum("pbt", [128, 1024], BF16)
    C = emit_consts(B, PS)
    B.phase("consts")
    scopes = {}
    modT_sb = [None, None]
    COMBT = [None, None]

    def open_modT():
        scopes["mod"] = ExitStack()
        for s_ in range(SEQ):
            modT_sb[s_] = B.sbuf(f"modT{s_}", [16, S], BF16, scopes["mod"])

    def open_combT():
        scopes["comb"] = ExitStack()
        for s_ in range(SEQ):
            COMBT[s_] = B.sbuf(f"combT{s_}", [8, S], BF16, scopes["comb"])

    for ph in phases:
        if ph == "ln0":
            emit_layernorm(B, C, PS, H["xT"], H["xn_h"], "ln0")
        elif ph == "mod0":
            open_modT()
            emit_mod(B, C, PS, H, modT_sb)
        elif ph == "attn0":
            for seq in range(SEQ):
                emit_attn_pairs(B, C, PS, H, f"a0s{seq}", seq, H["xn_h"],
                                H["l0a_wq"], H["l0a_bq_c"], H["l0a_wk"], H["l0a_bk_c"],
                                H["l0a_wv"], H["l0a_bv"], H["ctxa_h"], modT=modT_sb)
                B.phase(f"attn0s{seq}")
            scopes.pop("mod").close()
        elif ph == "ca0":
            for seq in range(SEQ):
                emit_attn_hd256(B, C, PS, f"c0s{seq}", seq, H["xn_h"],
                                H["l0a_cawin"], H["l0a_cabin"], H["l0a_cabin_c"], H["ctxb_h"])
                B.phase(f"ca0s{seq}")
        elif ph == "camix":
            emit_proj(B, C, PS, "cax", H["ctxb_h"], H["ctxb_h"], H["l0a_cawout"],
                      H["l0a_cabout_c"], mode="mix", res_h=H["ctxa_h"], w_res=0.3, w_new=0.7)
        elif ph == "mc0":
            for seq in range(SEQ):
                emit_attn_hd256(B, C, PS, f"m0s{seq}", seq, H["ctxb_h"],
                                H["l0a_mcwin"], H["l0a_mcbin"], H["l0a_mcbin_c"], H["ctxa_h"])
                B.phase(f"mc0s{seq}")
        elif ph == "mcmix":
            emit_proj(B, C, PS, "mcx", H["ctxa_h"], H["ctxa_h"], H["l0a_mcwout"],
                      H["l0a_mcbout_c"], mode="mix", res_h=H["ctxb_h"], w_res=0.85, w_new=0.15)
        elif ph == "wo0":
            emit_proj(B, C, PS, "wo0", H["ctxa_h"], H["s_h"], H["l0a_wo"],
                      H["l0a_bo_c"], mode="bias")
        elif ph == "rms1":
            emit_rms(B, C, PS, "rm1", H["xn_h"], H["y_h"], H["l0f_n1w_c"])
        elif ph == "ropeattn":
            for seq in range(SEQ):
                emit_attn_pairs(B, C, PS, H, f"rp{seq}", seq, H["y_h"],
                                H["l0f_qkvwT"][:, 0:D], H["l0f_qkvb_c"][:, 0:8],
                                H["l0f_qkvwT"][:, D:2 * D], H["l0f_qkvb_c"][:, 8:16],
                                H["l0f_qkvwT"][:, 2 * D:3 * D], H["l0f_qkvb"][2 * D:3 * D],
                                H["ctxb_h"], rope=True)
                B.phase(f"ropeattn{seq}")
        elif ph == "ropeproj":
            emit_proj(B, C, PS, "rpp", H["ctxb_h"], H["x1_h"], H["l0f_outwT"],
                      H["l0f_outb_c"], mode="bias_res", res_h=H["xn_h"])
        elif ph == "rms2":
            emit_rms(B, C, PS, "rm2", H["x1_h"], H["y_h"], H["l0f_n2w_c"])
        elif ph == "comb":
            open_combT()
            emit_comb(B, C, PS, H, COMBT)
        elif ph == "moe":
            emit_moe(B, C, PS, H, COMBT)
            scopes.pop("comb").close()
        elif ph == "ln1":
            emit_layernorm(B, C, PS, H["xr_h"], H["xn_h"], "ln1")
        elif ph == "attn1":
            for seq in range(SEQ):
                emit_attn_pairs(B, C, PS, H, f"a1s{seq}", seq, H["xn_h"],
                                H["l1a_wqT"], H["l1a_bq_c"], H["l1a_wkT"], H["l1a_bk_c"],
                                H["l1a_wvT"], H["l1a_bv"], H["ctxa_h"])
                B.phase(f"attn1s{seq}")
        elif ph == "wo1":
            emit_proj(B, C, PS, "wo1", H["ctxa_h"], H["s_h"], H["l1a_woT"],
                      H["l1a_bo_c"], mode="bias")
        elif ph == "ffn1":
            emit_ffn1(B, C, PS, H)
        else:
            raise ValueError(f"phase {ph} not implemented yet")
        B.phase(ph)

    emit_epilogue(B, H)
    B.finalize()
    return nc, B, H


def emit_epilogue(B, H):
    for sname, val in list(B.ctr.items()):
        if val > 0:
            B.wait("g", sname, val)


def kernel(**inputs):
    in_maps = host_layout(inputs)
    nc, B, H = build_program([p for p in ALL_PHASES])
    res = run_bass_kernel_spmd(nc, in_maps, core_ids=list(range(NCORE)))
    outs = []
    for c in range(NCORE):
        o = np.asarray(res.results[c]["outT"])
        outs.append(o.transpose(0, 2, 1))
    return np.ascontiguousarray(np.concatenate(outs, axis=0), dtype=np.float32)


# version 11 (final)
# speedup vs baseline: 1.3886x; 1.3886x over previous
import numpy as np
import ml_dtypes
from contextlib import ExitStack

import concourse.bass as bass
import concourse.mybir as mybir
from concourse.bass_utils import run_bass_kernel_spmd

F32 = mybir.dt.float32
BF16 = mybir.dt.bfloat16
I32 = mybir.dt.int32
AF = mybir.ActivationFunctionType
ALU = mybir.AluOpType
AX = mybir.AxisListType

NCORE = 8
B_FULL, S, D = 16, 2048, 1024
SEQ = 2
DC = D // 128
QT = 512
NQT = S // QT
KC = S // 128
C_SOFT = 20.0
LN_EPS = 1e-5
RMS_EPS = 1e-6

DEV = False

QUEUES = ("t", "v", "s", "g", "y")

ALL_PHASES = ["ln0", "mod0", "attn0", "ca0", "camix", "mc0", "mcmix", "wo0",
              "rms1", "ropeattn", "ropeproj", "rms2", "comb", "moe",
              "ln1", "attn1", "wo1", "ffn1"]


class Builder:
    def __init__(self, nc):
        self.nc = nc
        self.recs = {q: [] for q in QUEUES}
        self.ctx = ExitStack()
        self._sems = {}
        self.ctr = {}
        self.phase_touched = {}
        self.phase_names = []

    def sem(self, name):
        if name not in self._sems:
            self._sems[name] = self.ctx.enter_context(self.nc.semaphore(name))
            self.ctr[name] = 0
        return self._sems[name]

    def op(self, q, method, *args, inc=None, lp=False, **kw):
        incs = []
        if inc is not None:
            sname, n = inc
            self.sem(sname)
            self.ctr[sname] += n
            self.phase_touched[sname] = self.ctr[sname]
            incs.append((sname, n))
        if lp:
            kw["_lp"] = True
        self.recs[q].append((method, args, kw, incs))

    def wait(self, q, sname, val=None):
        self.sem(sname)
        if val is None:
            val = self.ctr[sname]
        if val > 0:
            self.recs[q].append(("wait_ge", (self._sems[sname], val), {}, []))

    def dma(self, q, out, in_, sname, **kw):
        self.op(q, "dma_start", inc=(sname, 16), out=out, in_=in_, **kw)

    def mark(self, sname):
        self.sem(sname)
        return self.ctr[sname]

    def phase(self, name=None):
        touched = {k: v for k, v in self.phase_touched.items() if v > 0}
        for q in QUEUES:
            for sname, val in touched.items():
                self.wait(q, sname, val)
        self.phase_touched = {}
        if name:
            self.phase_names.append(name)

    def sbuf(self, name, shape, dt, scope=None):
        scope = scope if scope is not None else self.ctx
        return scope.enter_context(self.nc.sbuf_tensor(name, shape, dt))

    def psum(self, name, shape, dt):
        return self.ctx.enter_context(self.nc.psum_tensor(name, shape, dt))

    def finalize(self):
        blk = self.ctx.enter_context(self.nc.Block())
        sems = self._sems

        def mk(q):
            def runner(eng):
                for ri, (method, args, kw, incs) in enumerate(self.recs[q]):
                    try:
                        if kw.get("_lp"):
                            kw2 = {k: v for k, v in kw.items() if k != "_lp"}
                            with self.nc.allow_low_precision(reason="intentional bf16 output"):
                                inst = getattr(eng, method)(*args, **kw2)
                        else:
                            inst = getattr(eng, method)(*args, **kw)
                    except Exception as e:
                        raise RuntimeError(f"queue {q} rec {ri} {method} "
                                           f"kw={list(kw)}: {e}") from e
                    for sname, n in incs:
                        inst.then_inc(sems[sname], n)
            return runner

        for q, attr in (("t", "tensor"), ("v", "vector"), ("s", "scalar"),
                        ("g", "gpsimd"), ("y", "sync")):
            if self.recs[q]:
                getattr(blk, attr)(mk(q))

    def n_instr(self):
        return {q: len(r) for q, r in self.recs.items()}


WEIGHT_SPECS = [
    ("xT", [SEQ, D, S], F32),
    ("l0a_wq", [D, D], F32), ("l0a_bq", [D], F32),
    ("l0a_wk", [D, D], F32), ("l0a_bk", [D], F32),
    ("l0a_wv", [D, D], F32), ("l0a_bv", [D], F32),
    ("l0a_cgw", [D, 16], F32), ("l0a_cgb", [16], F32),
    ("l0a_amw", [16, 16], F32), ("l0a_amb", [16], F32),
    ("l0a_cons", [16], F32),
    ("l0a_cawin", [D, 3 * D], F32), ("l0a_cabin", [3 * D], F32),
    ("l0a_cawout", [D, D], F32), ("l0a_cabout", [D], F32),
    ("l0a_mcwin", [D, 3 * D], F32), ("l0a_mcbin", [3 * D], F32),
    ("l0a_mcwout", [D, D], F32), ("l0a_mcbout", [D], F32),
    ("l0a_wo", [D, D], F32), ("l0a_bo", [D], F32),
    ("l0f_n1w", [D], F32), ("l0f_n2w", [D], F32),
    ("l0f_qkvwT", [D, 3 * D], F32), ("l0f_qkvb", [3 * D], F32),
    ("l0f_outwT", [D, D], F32), ("l0f_outb", [D], F32),
    ("l0f_gatewT", [D, 8], F32),
    ("l0f_w1T", [8, D, 2048], F32), ("l0f_b1", [8, 2048], F32),
    ("l0f_w2T", [8, D, 2048], F32), ("l0f_b2", [8, 2048], F32),
    ("l0f_w3T", [8, 2048, D], F32), ("l0f_b3", [8, D], F32),
    ("l1a_wqT", [D, D], F32), ("l1a_bq", [D], F32),
    ("l1a_wkT", [D, D], F32), ("l1a_bk", [D], F32),
    ("l1a_wvT", [D, D], F32), ("l1a_bv", [D], F32),
    ("l1a_woT", [D, D], F32), ("l1a_bo", [D], F32),
    ("l1f_wgT", [D, 3 * D], I32), ("l1f_wuT", [D, 3 * D], I32),
    ("l1f_wdT", [3 * D, D], I32),
    ("l0a_bq_c", [128, 8], F32), ("l0a_bk_c", [128, 8], F32),
    ("l0a_cabin_c", [128, 24], F32), ("l0a_mcbin_c", [128, 24], F32),
    ("l0a_cabout_c", [128, 8], F32), ("l0a_mcbout_c", [128, 8], F32),
    ("l0a_bo_c", [128, 8], F32),
    ("l0f_qkvb_c", [128, 24], F32), ("l0f_outb_c", [128, 8], F32),
    ("l0f_b1_c", [8, 128, 16], F32), ("l0f_b2_c", [8, 128, 16], F32),
    ("l0f_b3_c", [8, 128, 8], F32),
    ("l0f_n1w_c", [128, 8], F32), ("l0f_n2w_c", [128, 8], F32),
    ("l1a_bq_c", [128, 8], F32), ("l1a_bk_c", [128, 8], F32),
    ("l1a_bo_c", [128, 8], F32),
]

SCRATCH_SPECS = [
    ("xn_h", [SEQ, D, S], BF16),
    ("s_h", [SEQ, D, S], BF16),
    ("x1_h", [SEQ, D, S], BF16),
    ("y_h", [SEQ, D, S], BF16),
    ("ctxa_h", [SEQ, D, S], BF16),
    ("ctxb_h", [SEQ, D, S], BF16),
    ("acc_h", [SEQ, D, S], F32),
    ("xr_h", [SEQ, D, S], F32),
    ("combT_h", [SEQ, 8, S], F32),
]


def declare_hbm(nc):
    H = {}
    for name, shape, dt in WEIGHT_SPECS:
        H[name] = nc.dram_tensor(name, shape, dt, kind="ExternalInput").ap()
    H["outT"] = nc.dram_tensor("outT", [SEQ, D, S], F32, kind="ExternalOutput").ap()
    skind = "ExternalOutput" if DEV else "Internal"
    for name, shape, dt in SCRATCH_SPECS:
        H[name] = nc.dram_tensor(name, shape, dt, kind=skind).ap()
    return H


class Consts:
    pass


def emit_consts(B, PS):
    C = Consts()
    C.ident = B.sbuf("c_ident", [128, 128], BF16)
    C.ones_row_bf = B.sbuf("c_ones_bf", [1, 128], BF16)
    C.ones_row_f = B.sbuf("c_ones_f", [1, 128], F32)
    C.ones_col_bf = B.sbuf("c_onescol_bf", [128, 1], BF16)
    C.ones_col_f = B.sbuf("c_onescol_f", [128, 1], F32)
    C.negc = B.sbuf("c_negc", [128, 1], F32)
    C.eps_ln = B.sbuf("c_epsln", [1, 1], F32)
    C.eps_rms = B.sbuf("c_epsrms", [1, 1], F32)
    C.pairsel = [B.sbuf(f"c_pairsel{hp}", [16, 128], BF16) for hp in range(8)]
    C.esel = [B.sbuf(f"c_esel{e}", [8, 128], BF16) for e in range(8)]
    C.hsel2 = B.sbuf("c_hsel2", [2, 128], BF16)
    C.rcos = B.sbuf("c_rcos", [64, S], BF16)
    C.rsin = B.sbuf("c_rsin", [64, S], BF16)

    with ExitStack() as sc:
        it_pf = B.sbuf("c_it_pf", [128, 128], I32, sc)
        it_sel = B.sbuf("c_it_sel", [16, 128], I32, sc)
        it_e = B.sbuf("c_it_e", [8, 128], I32, sc)
        it_inv = B.sbuf("c_it_inv", [1, 64], I32, sc)
        it_pos = B.sbuf("c_it_pos", [1, S], I32, sc)
        f_inv = B.sbuf("c_f_inv", [1, 64], F32, sc)
        f_pos = B.sbuf("c_f_pos", [1, S], F32, sc)
        signc = B.sbuf("c_signc", [64, 1], F32, sc)
        th_f = B.sbuf("c_th_f", [64, 512], F32, sc)
        th_i = B.sbuf("c_th_i", [64, 512], I32, sc)
        th_r = B.sbuf("c_th_r", [64, 512], F32, sc)

        B.op("g", "iota", it_pf[:], [[1, 128]], base=0, channel_multiplier=-1, inc=("cst", 1))
        B.op("g", "iota", it_sel[:], [[-1, 2], [0, 64]], base=0, channel_multiplier=1, inc=("cst", 1))
        B.op("g", "iota", it_e[:], [[0, 128]], base=0, channel_multiplier=1, inc=("cst", 1))
        B.op("g", "iota", it_inv[:], [[0, 2], [1, 32]], channel_multiplier=0, inc=("cst", 1))
        B.op("g", "iota", it_pos[:], [[1, S]], channel_multiplier=0, inc=("cst", 1))
        B.op("g", "memset", C.ones_row_bf[:], 1.0, inc=("cst", 1))
        B.op("g", "memset", C.ones_row_f[:], 1.0, inc=("cst", 1))
        B.op("g", "memset", C.ones_col_bf[:], 1.0, inc=("cst", 1))
        B.op("g", "memset", C.ones_col_f[:], 1.0, inc=("cst", 1))
        B.op("g", "memset", C.negc[:], -C_SOFT, inc=("cst", 1))
        B.op("g", "memset", C.eps_ln[:], LN_EPS, inc=("cst", 1))
        B.op("g", "memset", C.eps_rms[:], RMS_EPS, inc=("cst", 1))
        B.op("g", "memset", signc[:], 1.0, inc=("cst", 1))
        B.wait("g", "cst", B.mark("cst"))
        B.op("g", "memset", signc[32:64, :], -1.0, inc=("cst", 1))

        B.wait("v", "cst")
        B.op("v", "tensor_single_scalar", C.ident[:], it_pf[:], 0.0, ALU.is_equal, inc=("cstv", 1))
        for hp in range(8):
            B.op("v", "tensor_single_scalar", C.pairsel[hp][:], it_sel[:], float(2 * hp), ALU.is_equal, inc=("cstv", 1))
            B.wait("v", "cstv")
            B.op("v", "tensor_scalar_mul", C.pairsel[hp][:], C.pairsel[hp][:], 0.125, inc=("cstv", 1))
        for e in range(8):
            B.op("v", "tensor_single_scalar", C.esel[e][:], it_e[:], float(e), ALU.is_equal, inc=("cstv", 1))
        B.op("v", "tensor_single_scalar", C.hsel2[:], it_sel[0:2, :], 0.0, ALU.is_equal, inc=("cstv", 1))
        B.op("v", "tensor_copy", f_inv[:], it_inv[:], inc=("cstv", 1))
        B.op("v", "tensor_copy", f_pos[:], it_pos[:], inc=("cstv", 1))

        B.wait("s", "cstv")
        B.op("s", "activation", f_inv[:], f_inv[:], AF.Exp,
             scale=float(-np.log(10000.0) / 32.0), inc=("cstf", 1))

        for qt in range(NQT):
            ts = slice(qt * 512, (qt + 1) * 512)
            B.wait("t", "cstf")
            if qt > 0:
                B.wait("t", "rope_rd", B.mark("rope_rd"))
            B.op("t", "matmul", PS.b[0][0:64, :], f_inv[:], f_pos[:, ts],
                 start=True, stop=True, inc=("rope_mm", 1))
            for is_cos in (0, 1):
                B.wait("v", "rope_mm", B.mark("rope_mm"))
                if is_cos == 0:
                    B.op("v", "tensor_scalar_mul", th_f[:], PS.b[0][0:64, :],
                         float(1.0 / (2 * np.pi)), inc=("rope_v", 1))
                else:
                    B.op("v", "tensor_scalar", th_f[:], PS.b[0][0:64, :],
                         float(1.0 / (2 * np.pi)), 0.25, ALU.mult, ALU.add, inc=("rope_rd", 1))
                B.op("v", "tensor_copy", th_i[:], th_f[:])
                B.op("v", "tensor_copy", th_r[:], th_i[:])
                B.op("v", "tensor_tensor", th_r[:], th_f[:], th_r[:], ALU.subtract)
                B.op("v", "tensor_scalar_mul", th_r[:], th_r[:], float(2 * np.pi), inc=("rope_v", 1))
                B.wait("s", "rope_v", B.mark("rope_v"))
                dst = (C.rcos if is_cos else C.rsin)[:, ts]
                B.op("s", "activation", dst, th_r[:], AF.Sin, inc=("rope_s", 1))
                B.wait("v", "rope_s", B.mark("rope_s"))
        B.op("v", "tensor_scalar_mul", C.rsin[:], C.rsin[:], signc[:, 0:1], inc=("cstv", 1))
    return C


def emit_layernorm(B, C, PS, src_h, dst_h, tag):
    with ExitStack() as sc:
        xf = [[B.sbuf(f"{tag}_xf{b}_{c}", [128, QT], F32, sc) for c in range(DC)] for b in range(2)]
        xb = [[B.sbuf(f"{tag}_xb{b}_{c}", [128, QT], BF16, sc) for c in range(DC)] for b in range(2)]
        sq = [[B.sbuf(f"{tag}_sq{b}_{c}", [128, QT], BF16, sc) for c in range(DC)] for b in range(2)]
        r_mean = [B.sbuf(f"{tag}_rmean{b}", [1, QT], F32, sc) for b in range(2)]
        r_msq = B.sbuf(f"{tag}_rmsq", [1, QT], F32, sc)
        r_var = B.sbuf(f"{tag}_rvar", [1, QT], F32, sc)
        r_std = B.sbuf(f"{tag}_rstd", [1, QT], F32, sc)
        r_rstd = [B.sbuf(f"{tag}_rrstd{b}", [1, QT], F32, sc) for b in range(2)]
        rmean_b = [B.sbuf(f"{tag}_rmeanb{b}", [1, QT], BF16, sc) for b in range(2)]
        rrstd_b = [B.sbuf(f"{tag}_rrstdb{b}", [1, QT], BF16, sc) for b in range(2)]
        tmp = [B.sbuf(f"{tag}_tmp{i}", [128, QT], F32, sc) for i in range(2)]
        xout = [B.sbuf(f"{tag}_xo{i}", [128, QT], BF16, sc) for i in range(2)]
        ld, sqd, mmd, rowd, bcd, evd, spl = "ld", "sq", "mm", "row", "bc", "ev", "sp"
        sp_marks = []
        ev_guard = [0, 0]
        row_guard = [0, 0]
        it = 0
        for seq in range(SEQ):
            for qt in range(NQT):
                ts = slice(qt * QT, (qt + 1) * QT)
                p = it % 2
                stats = PS.A[p][0:1, :]
                bme, brs = PS.b[2 * p], PS.b[2 * p + 1]
                if ev_guard[p]:
                    B.wait("y", evd, ev_guard[p])
                for c in range(DC):
                    B.dma("y", xf[p][c][:], src_h[seq, c * 128:(c + 1) * 128, ts], ld)
                ldv = B.mark(ld)
                B.wait("s", ld, ldv)
                for c in range(DC):
                    B.op("s", "activation", sq[p][c][:], xf[p][c][:], AF.Square, inc=(sqd, 1))
                B.wait("v", ld, ldv)
                for c in range(DC):
                    B.op("v", "tensor_copy", xb[p][c][:], xf[p][c][:], inc=(sqd, 1))
                B.wait("t", sqd, B.mark(sqd))
                if row_guard[p]:
                    B.wait("t", rowd, row_guard[p])
                for c in range(DC):
                    B.op("t", "matmul", stats[:, 0:512], C.ones_col_bf[:], xb[p][c][:],
                         start=(c == 0), stop=(c == DC - 1),
                         inc=((mmd, 1) if c == DC - 1 else None))
                for c in range(DC):
                    B.op("t", "matmul", stats[:, 512:1024], C.ones_col_bf[:], sq[p][c][:],
                         start=(c == 0), stop=(c == DC - 1),
                         inc=((mmd, 1) if c == DC - 1 else None))
                B.wait("v", mmd, B.mark(mmd))
                B.op("v", "tensor_scalar_mul", r_mean[p][:], stats[:, 0:512], float(1.0 / D))
                B.op("v", "tensor_scalar_mul", r_msq[:], stats[:, 512:1024], float(1.0 / D))
                B.op("v", "tensor_tensor", r_var[:], r_mean[p][:], r_mean[p][:], ALU.mult)
                B.op("v", "tensor_tensor", r_var[:], r_msq[:], r_var[:], ALU.subtract, inc=(rowd, 1))
                row_guard[p] = B.mark(rowd)
                B.wait("s", rowd, B.mark(rowd))
                B.op("s", "activation", r_std[:], r_var[:], AF.Sqrt, bias=C.eps_ln[:, 0:1], inc=(rowd, 1))
                B.wait("v", rowd, B.mark(rowd))
                B.op("v", "reciprocal", r_rstd[p][:], r_std[:])
                B.op("v", "tensor_copy", rmean_b[p][:], r_mean[p][:])
                B.op("v", "tensor_copy", rrstd_b[p][:], r_rstd[p][:], inc=(rowd, 1))
                B.wait("t", rowd, B.mark(rowd))
                if ev_guard[p]:
                    B.wait("t", evd, ev_guard[p])
                B.op("t", "matmul", bme[:], C.ones_row_bf[0:1, :], rmean_b[p][:], start=True, stop=True, inc=(bcd, 1))
                B.op("t", "matmul", brs[:], C.ones_row_bf[0:1, :], rrstd_b[p][:], start=True, stop=True, inc=(bcd, 1))
                B.wait("v", bcd, B.mark(bcd))
                for c in range(DC):
                    buf = c % 2
                    if len(sp_marks) >= 2:
                        B.wait("v", spl, sp_marks[-2])
                    B.op("v", "tensor_tensor", tmp[buf][:], xf[p][c][:], bme[:], ALU.subtract)
                    B.op("v", "tensor_tensor", xout[buf][:], tmp[buf][:], brs[:], ALU.mult, inc=(evd, 1))
                    B.wait("y", evd, B.mark(evd))
                    B.dma("y", dst_h[seq, c * 128:(c + 1) * 128, ts], xout[buf][:], spl)
                    sp_marks.append(B.mark(spl))
                ev_guard[p] = B.mark(evd)
                it += 1


def col_f32(ap1d, a, b):
    return ap1d[a:b].rearrange("(p o) -> p o", o=1)


def row_f32(ap1d, a, b):
    return ap1d[a:b].rearrange("(o p) -> o p", o=1)


def emit_mod(B, C, PS, H, modT_sb):
    with ExitStack() as sc:
        cgw = B.sbuf("md_cgw", [128, 8 * 16], BF16, sc)
        amw = B.sbuf("md_amw", [16, 16], BF16, sc)
        cons = B.sbuf("md_cons", [16, 1], BF16, sc)
        cgb = B.sbuf("md_cgb", [16, 1], F32, sc)
        amb = B.sbuf("md_amb", [16, 1], F32, sc)
        biasc = B.sbuf("md_biasc", [16, 1], F32, sc)
        xt = [B.sbuf(f"md_x{c}", [128, QT], BF16, sc) for c in range(DC)]
        for c in range(DC):
            B.dma("g", cgw[:, c * 16:(c + 1) * 16], H["l0a_cgw"][c * 128:(c + 1) * 128, :], "mdw")
        B.dma("g", amw[:], H["l0a_amw"], "mdw")
        B.dma("g", cons[:], col_f32(H["l0a_cons"], 0, 16), "mdw")
        B.dma("y", cgb[:], col_f32(H["l0a_cgb"], 0, 16), "mdb")
        B.dma("y", amb[:], col_f32(H["l0a_amb"], 0, 16), "mdb")
        wv_ = B.mark("mdw")
        B.wait("t", "mdw", wv_)
        B.wait("v", "mdb", B.mark("mdb"))
        B.op("t", "matmul", PS.b[2][0:16, 0:1], amw[:], cons[:], start=True, stop=True, inc=("mdm", 1))
        B.wait("v", "mdm", B.mark("mdm"))
        B.op("v", "tensor_tensor", biasc[:], PS.b[2][0:16, 0:1], cgb[:], ALU.add)
        B.op("v", "tensor_tensor", biasc[:], biasc[:], amb[:], ALU.add, inc=("mdv", 1))
        for seq in range(SEQ):
            for qt in range(NQT):
                ts = slice(qt * QT, (qt + 1) * QT)
                it = seq * NQT + qt
                if it > 0:
                    B.wait("y", "mds", B.mark("mds"))
                for c in range(DC):
                    B.dma("y", xt[c][:], H["xn_h"][seq, c * 128:(c + 1) * 128, ts], "mdx")
                B.wait("t", "mdx", B.mark("mdx"))
                if it > 0:
                    B.wait("t", "mds", B.mark("mds"))
                for c in range(DC):
                    B.op("t", "matmul", PS.b[0][0:16, :], cgw[:, c * 16:(c + 1) * 16], xt[c][:],
                         start=(c == 0), stop=(c == DC - 1),
                         inc=(("mdm", 1) if c == DC - 1 else None))
                B.wait("s", "mdm", B.mark("mdm"))
                B.wait("s", "mdv", 1)
                B.op("s", "activation", modT_sb[seq][:, ts], PS.b[0][0:16, :], AF.Sigmoid,
                     bias=biasc[:, 0:1], inc=("mds", 1))


def emit_attn_pairs(B, C, PS, H, tag, seq, src, wq, bq_c, wk, bk_c, wv, bv,
                    ctx_dst, modT=None, rope=False, exp_scale=0.125):
    with ExitStack() as sc:
        xn_sb = B.sbuf(f"{tag}_xn", [128, 16384], BF16, sc)
        vt_sb = B.sbuf(f"{tag}_vt", [128, 16384], BF16, sc)
        wv_sb = B.sbuf(f"{tag}_wv", [128, 8192], BF16, sc)
        wqp = [B.sbuf(f"{tag}_wqp{i}", [128, 1024], BF16, sc) for i in range(2)]
        wkp = [B.sbuf(f"{tag}_wkp{i}", [128, 1024], BF16, sc) for i in range(2)]
        qtp = [B.sbuf(f"{tag}_qtp{i}", [128, S], BF16, sc) for i in range(2)]
        ktp = [B.sbuf(f"{tag}_ktp{i}", [128, S], BF16, sc) for i in range(2)]
        probs = [B.sbuf(f"{tag}_pr{i}", [128, 16384], BF16, sc) for i in range(2)]
        bvbc = B.sbuf(f"{tag}_bvbc", [128, 1024], F32, sc)
        bvrow = B.sbuf(f"{tag}_bvrow", [1, 1024], BF16, sc)
        bqc8 = B.sbuf(f"{tag}_bqc", [128, 8], F32, sc)
        bkc8 = B.sbuf(f"{tag}_bkc", [128, 8], F32, sc)
        modbc = [B.sbuf(f"{tag}_modbc{i}", [128, QT], F32, sc) for i in range(2)] if modT is not None else None
        qtmp = B.sbuf(f"{tag}_qtmp", [128, QT], F32, sc) if modT is not None else None
        rec = [B.sbuf(f"{tag}_rec{h}", [1, QT], F32, sc) for h in range(2)]
        ctxu = B.sbuf(f"{tag}_ctxu", [128, QT], BF16, sc)
        ctxo = B.sbuf(f"{tag}_ctxo", [128, QT], BF16, sc)
        rst = [B.sbuf(f"{tag}_rst{i}", [64, QT], F32, sc) for i in range(3)] if rope else None

        ld, wl, mm, ex, pv, nr, ev, sp = "ld", "wl", "mm", "ex", "pv", "nr", "ev", "sp"
        if modT is not None:
            exp_scale = 1.0
        ld_marks = {}
        for qq in range(NQT):
            for c in range(DC):
                B.dma("y", xn_sb[:, c * S + qq * QT:c * S + (qq + 1) * QT],
                      src[seq, c * 128:(c + 1) * 128, qq * QT:(qq + 1) * QT], ld)
            ld_marks[qq] = B.mark(ld)
        for c in range(DC):
            B.dma("g", wv_sb[:, c * 1024:(c + 1) * 1024], wv[c * 128:(c + 1) * 128, :], wl)
        B.dma("g", bvrow[:], row_f32(bv, 0, 1024), wl)
        B.dma("y", bqc8[:], bq_c, "bl")
        B.dma("y", bkc8[:], bk_c, "bl")
        B.wait("t", wl, B.mark(wl))
        B.wait("v", "bl", B.mark("bl"))
        B.op("t", "matmul", PS.b[0][:], C.ones_row_bf[:], bvrow[:, 0:512], start=True, stop=True, inc=(mm, 1))
        B.op("t", "matmul", PS.b[1][:], C.ones_row_bf[:], bvrow[:, 512:1024], start=True, stop=True, inc=(mm, 1))
        B.wait("s", mm, B.mark(mm))
        B.op("s", "copy", bvbc[:, 0:512], PS.b[0][:], inc=(nr, 1))
        B.op("s", "copy", bvbc[:, 512:1024], PS.b[1][:], inc=(nr, 1))
        B.wait("t", nr, B.mark(nr))
        vev = []
        vi = 0
        for tt in range(16):
            for grp in range(2):
                bank = PS.b[vi % 2]
                if grp == 0:
                    B.wait("t", ld, ld_marks[tt // 4])
                if vi >= 2:
                    B.wait("t", ev, vev[vi - 2])
                for c in range(DC):
                    B.op("t", "matmul", bank[:],
                         xn_sb[:, c * S + tt * 128:c * S + (tt + 1) * 128],
                         wv_sb[:, c * 1024 + grp * 512:c * 1024 + (grp + 1) * 512],
                         start=(c == 0), stop=(c == DC - 1),
                         inc=((mm, 1) if c == DC - 1 else None))
                B.wait("v", mm, B.mark(mm))
                B.op("v", "tensor_tensor",
                     vt_sb[:, tt * 1024 + grp * 512:tt * 1024 + (grp + 1) * 512],
                     bank[:], bvbc[:, grp * 512:(grp + 1) * 512], ALU.add, inc=(ev, 1))
                vev.append(B.mark(ev))
                vi += 1
        st = {"wlm": {}, "exm": {}, "pvm": {}, "projev": {},
              "b2free": 0, "b3free": 0, "b0free": B.mark(ev), "recnr": 0,
              "ctxev": B.mark(ev)}

        def emit_wload(hp, guard):
            B.wait("g", mm, guard)
            for c in range(DC):
                B.dma("g", wqp[hp % 2][:, c * 128:(c + 1) * 128],
                      wq[c * 128:(c + 1) * 128, hp * 128:(hp + 1) * 128], wl)
                B.dma("g", wkp[hp % 2][:, c * 128:(c + 1) * 128],
                      wk[c * 128:(c + 1) * 128, hp * 128:(hp + 1) * 128], wl)
            st["wlm"][hp] = B.mark(wl)

        def emit_proj_slice(hp, qt):
            bufp = hp % 2
            ts = slice(qt * QT, (qt + 1) * QT)
            bqc = bqc8[:, hp:hp + 1]
            bkc = bkc8[:, hp:hp + 1]
            B.wait("t", wl, st["wlm"][hp])
            B.wait("t", ld, ld_marks[qt])
            if st["b2free"]:
                B.wait("t", ev, st["b2free"])
            if modT is not None:
                B.wait("t", nr, B.mark(nr))
                B.op("t", "matmul", PS.b[2][:], C.pairsel[hp][:], modT[seq][:, ts],
                     start=True, stop=True, inc=(mm, 1))
                B.wait("s", mm, B.mark(mm))
                B.op("s", "copy", modbc[bufp][:], PS.b[2][:], inc=(nr, 1))
                B.wait("t", nr, B.mark(nr))
            for c in range(DC):
                B.op("t", "matmul", PS.b[2][:], wqp[bufp][:, c * 128:(c + 1) * 128],
                     xn_sb[:, c * S + qt * QT:c * S + (qt + 1) * QT],
                     start=(c == 0), stop=(c == DC - 1),
                     inc=((mm, 1) if c == DC - 1 else None))
            if st["b3free"]:
                B.wait("t", ev, st["b3free"])
            for c in range(DC):
                B.op("t", "matmul", PS.b[3][:], wkp[bufp][:, c * 128:(c + 1) * 128],
                     xn_sb[:, c * S + qt * QT:c * S + (qt + 1) * QT],
                     start=(c == 0), stop=(c == DC - 1),
                     inc=((mm, 1) if c == DC - 1 else None))
            mmv = B.mark(mm)
            B.wait("v", mm, mmv)
            if modT is not None:
                B.wait("v", nr, B.mark(nr))
            if rope:
                emit_rope_pair(B, C, PS.b[2], bqc8[:, hp:hp + 1], qtp[bufp], ts, rst, ev)
                st["b2free"] = B.mark(ev)
                emit_rope_pair(B, C, PS.b[3], bkc8[:, hp:hp + 1], ktp[bufp], ts, rst, ev)
                st["b3free"] = B.mark(ev)
            else:
                if modT is not None:
                    B.op("v", "tensor_scalar_add", qtmp[:], PS.b[2][:], bqc)
                    B.op("v", "tensor_tensor", qtp[bufp][:, ts], qtmp[:], modbc[bufp][:],
                         ALU.mult, inc=(ev, 1))
                else:
                    B.op("v", "tensor_scalar_add", qtp[bufp][:, ts], PS.b[2][:], bqc, inc=(ev, 1))
                st["b2free"] = B.mark(ev)
                B.op("v", "tensor_scalar_add", ktp[bufp][:, ts], PS.b[3][:], bkc, inc=(ev, 1))
                st["b3free"] = B.mark(ev)
            st["projev"][(hp, qt)] = B.mark(ev)

        def emit_scores(hp, qt):
            bufp = hp % 2
            ts = slice(qt * QT, (qt + 1) * QT)
            B.wait("t", ev, st["projev"][(hp, qt)])
            for kc in range(KC):
                buf = PS.A[kc % 2]
                key = None
                if kc % 2 == 0:
                    if kc >= 2:
                        key = (hp, qt, kc - 1)
                    elif qt > 0:
                        key = (hp, qt - 1, KC - 1)
                    elif hp > 0:
                        key = (hp - 1, NQT - 1, KC - 1)
                if key is not None and key in st["exm"]:
                    B.wait("t", ex, st["exm"][key])
                B.op("t", "matmul", buf[:, 0:512], ktp[bufp][0:64, kc * 128:(kc + 1) * 128],
                     qtp[bufp][0:64, ts], start=True, stop=True, tile_position=(0, 0), inc=(mm, 1))
                B.op("t", "matmul", buf[:, 512:1024], ktp[bufp][64:128, kc * 128:(kc + 1) * 128],
                     qtp[bufp][64:128, ts], start=True, stop=True, tile_position=(64, 0), inc=(mm, 1))
                B.wait("s", mm, B.mark(mm))
                if kc == 0:
                    pk = st["pvm"].get((hp, qt - 2)) if qt >= 2 else (
                        st["pvm"].get((hp - 1, NQT - 2 + qt)) if hp > 0 else None)
                    if pk:
                        B.wait("s", pv, pk)
                B.op("s", "activation", probs[qt % 2][:, kc * 1024:(kc + 1) * 1024], buf[:],
                     AF.Exp, bias=C.negc[:, 0:1], scale=exp_scale, inc=(ex, 1))
                st["exm"][(hp, qt, kc)] = B.mark(ex)

        def emit_pv(hp, qt):
            for kc in range(KC):
                B.wait("t", ex, st["exm"][(hp, qt, kc)])
                if kc == 0 and st["ctxev"]:
                    B.wait("t", ev, st["ctxev"])
                for h in range(2):
                    B.op("t", "matmul", PS.b[1][h * 64:h * 64 + 1, :],
                         C.ones_col_bf[:],
                         probs[qt % 2][:, kc * 1024 + h * 512:kc * 1024 + (h + 1) * 512],
                         start=(kc == 0), stop=(kc == KC - 1),
                         tile_position=(0, h * 64), skip_group_check=True,
                         inc=((pv, 1) if kc == KC - 1 and h == 1 else None))
            sums_mark = B.mark(pv)
            B.wait("v", pv, sums_mark)
            B.op("v", "reciprocal", rec[0][:], PS.b[1][0:1, :])
            B.op("v", "reciprocal", rec[1][:], PS.b[1][64:65, :], inc=(nr, 1))
            st["recnr"] = B.mark(nr)
            for kc in range(KC):
                if kc == 0 and st["b0free"]:
                    B.wait("t", ev, st["b0free"])
                for h in range(2):
                    hg = 2 * hp + h
                    B.op("t", "matmul", PS.b[0][h * 64:(h + 1) * 64, :],
                         vt_sb[:, kc * 1024 + hg * 64:kc * 1024 + (hg + 1) * 64],
                         probs[qt % 2][:, kc * 1024 + h * 512:kc * 1024 + (h + 1) * 512],
                         start=(kc == 0), stop=(kc == KC - 1),
                         tile_position=(0, h * 64), skip_group_check=True,
                         inc=((pv, 1) if kc == KC - 1 and h == 1 else None))
            st["pvm"][(hp, qt)] = B.mark(pv)

        def emit_evac(hp, qt):
            B.wait("v", pv, st["pvm"][(hp, qt)])
            B.op("v", "tensor_copy", ctxu[:], PS.b[0][:], lp=True, inc=(ev, 1))
            st["b0free"] = B.mark(ev)

        def emit_norm(hp, qt):
            ts = slice(qt * QT, (qt + 1) * QT)
            B.wait("t", nr, st["recnr"])
            B.op("t", "matmul", PS.b[1][0:64, :], C.ones_row_f[0:1, 0:64], rec[0][:],
                 start=True, stop=True, tile_position=(0, 0), inc=(mm, 1))
            B.op("t", "matmul", PS.b[1][64:128, :], C.ones_row_f[0:1, 0:64], rec[1][:],
                 start=True, stop=True, tile_position=(0, 64), inc=(mm, 1))
            B.wait("v", mm, B.mark(mm))
            if st.get("spm"):
                B.wait("v", sp, st["spm"])
            B.op("v", "tensor_tensor", ctxo[:], ctxu[:], PS.b[1][:], ALU.mult, inc=(ev, 1))
            st["ctxev"] = B.mark(ev)
            B.wait("y", ev, st["ctxev"])
            B.dma("y", ctx_dst[seq, hp * 128:(hp + 1) * 128, ts], ctxo[:], sp)
            st["spm"] = B.mark(sp)

        emit_wload(0, 0)
        for qt in range(NQT):
            emit_proj_slice(0, qt)
        for hp in range(8):
            if hp < 7:
                emit_wload(hp + 1, B.mark(mm))
            for qt in range(NQT):
                emit_scores(hp, qt)
                if qt > 0:
                    emit_pv(hp, qt - 1)
                    emit_evac(hp, qt - 1)
                if hp < 7:
                    emit_proj_slice(hp + 1, qt)
                if qt > 0:
                    emit_norm(hp, qt - 1)
            emit_pv(hp, NQT - 1)
            emit_evac(hp, NQT - 1)
            emit_norm(hp, NQT - 1)


def emit_rope_pair(B, C, psum, bias_col, out_pair, ts, rst, ev):
    st, st2, st3 = rst
    for h in range(2):
        hb = h * 64
        B.op("v", "tensor_scalar_add", st[:], psum[hb:hb + 64, :], bias_col[hb:hb + 64, :])
        B.op("v", "tensor_tensor", st3[:], st[:], C.rcos[:, ts], ALU.mult)
        B.op("v", "tensor_tensor", st2[0:32, :], st[32:64, :], C.rsin[32:64, ts], ALU.mult)
        B.op("v", "tensor_tensor", st2[32:64, :], st[0:32, :], C.rsin[0:32, ts], ALU.mult)
        B.op("v", "tensor_tensor", st3[:], st3[:], st2[:], ALU.add)
        B.op("v", "tensor_copy", out_pair[hb:hb + 64, ts], st3[:], inc=(ev, 1))


def emit_proj(B, C, PS, tag, src_h, dst_h, wT, bias_c, mode="bias",
              res_h=None, w_res=0.0, w_new=1.0, dst_dt=BF16):
    with ExitStack() as sc:
        wsb = B.sbuf(f"{tag}_w", [128, 8192], BF16, sc)
        bcol = B.sbuf(f"{tag}_b", [128, 8], F32, sc)
        xt = [[B.sbuf(f"{tag}_x{b}_{c}", [128, QT], BF16, sc) for c in range(DC)] for b in range(2)]
        rt = [B.sbuf(f"{tag}_r{i}", [128, QT], BF16, sc) for i in range(2)]
        ot = [B.sbuf(f"{tag}_o{i}", [128, QT], dst_dt, sc) for i in range(2)]
        tmp = B.sbuf(f"{tag}_tmp", [128, QT], F32, sc)
        ld, wl, mm, ev, sp = "ld", "wl", "mm", "ev", "sp"
        for kc in range(DC):
            B.dma("g", wsb[:, kc * 1024:(kc + 1) * 1024], wT[kc * 128:(kc + 1) * 128, :], wl)
        B.dma("y", bcol[:], bias_c, "bl")
        blv = B.mark("bl")
        wlv = B.mark(wl)
        sp_marks = []
        ev_marks = []
        xt_guard = [0, 0]
        it = 0
        for seq in range(SEQ):
            for qt in range(NQT):
                ts = slice(qt * QT, (qt + 1) * QT)
                xb = xt[it % 2]
                if xt_guard[it % 2]:
                    B.wait("y", mm, xt_guard[it % 2])
                for c in range(DC):
                    B.dma("y", xb[c][:], src_h[seq, c * 128:(c + 1) * 128, ts], ld)
                B.wait("t", ld, B.mark(ld))
                B.wait("t", wl, wlv)
                B.wait("v", "bl", blv)
                for m in range(DC):
                    bank = PS.b[m % 2]
                    bufi = m % 2
                    if mode != "bias":
                        if len(ev_marks) >= 2:
                            B.wait("y", ev, ev_marks[-2])
                        B.dma("y", rt[bufi][:], res_h[seq, m * 128:(m + 1) * 128, ts], ld)
                    if len(ev_marks) >= 2:
                        B.wait("t", ev, ev_marks[-2])
                    for kc in range(DC):
                        B.op("t", "matmul", bank[:], wsb[:, kc * 1024 + m * 128:kc * 1024 + (m + 1) * 128],
                             xb[kc][:], start=(kc == 0), stop=(kc == DC - 1),
                             inc=((mm, 1) if kc == DC - 1 else None))
                    B.wait("v", mm, B.mark(mm))
                    if mode != "bias":
                        B.wait("v", ld, B.mark(ld))
                    if len(sp_marks) >= 2:
                        B.wait("v", sp, sp_marks[-2])
                    if mode == "bias":
                        B.op("v", "tensor_scalar_add", ot[bufi][:], bank[:], bcol[:, m:m + 1], inc=(ev, 1))
                    elif mode == "mix":
                        B.op("v", "tensor_scalar", tmp[:], bank[:], bcol[:, m:m + 1], w_new,
                             ALU.add, ALU.mult)
                        B.op("v", "scalar_tensor_tensor", ot[bufi][:], rt[bufi][:], w_res, tmp[:],
                             ALU.mult, ALU.add, inc=(ev, 1))
                    elif mode == "bias_res":
                        B.op("v", "tensor_scalar_add", tmp[:], bank[:], bcol[:, m:m + 1])
                        B.op("v", "tensor_tensor", ot[bufi][:], tmp[:], rt[bufi][:], ALU.add, inc=(ev, 1))
                    ev_marks.append(B.mark(ev))
                    B.wait("y", ev, B.mark(ev))
                    B.dma("y", dst_h[seq, m * 128:(m + 1) * 128, ts], ot[bufi][:], sp)
                    sp_marks.append(B.mark(sp))
                xt_guard[it % 2] = B.mark(mm)
                it += 1


def emit_rms(B, C, PS, tag, src_h, dst_h, w_c):
    with ExitStack() as sc:
        xf = [[B.sbuf(f"{tag}_xf{b}_{c}", [128, QT], BF16, sc) for c in range(DC)] for b in range(2)]
        sq = [[B.sbuf(f"{tag}_sq{b}_{c}", [128, QT], BF16, sc) for c in range(DC)] for b in range(2)]
        wcol = B.sbuf(f"{tag}_wcol", [128, 8], F32, sc)
        r_msq = B.sbuf(f"{tag}_rmsq", [1, QT], F32, sc)
        r_scr = B.sbuf(f"{tag}_rscr", [1, QT], F32, sc)
        r_std = B.sbuf(f"{tag}_rstd", [1, QT], F32, sc)
        r_rstd = [B.sbuf(f"{tag}_rrstd{b}", [1, QT], BF16, sc) for b in range(2)]
        r_rstdf = B.sbuf(f"{tag}_rrstdf", [1, QT], F32, sc)
        xout = [B.sbuf(f"{tag}_xo{i}", [128, QT], BF16, sc) for i in range(2)]
        ld, sqd, mmd, rowd, bcd, evd, spl = "ld", "sq", "mm", "row", "bc", "ev", "sp"
        B.dma("y", wcol[:], w_c, ld)
        sp_marks = []
        ev_guard = [0, 0]
        row_guard = [0, 0]
        it = 0
        for seq in range(SEQ):
            for qt in range(NQT):
                ts = slice(qt * QT, (qt + 1) * QT)
                p = it % 2
                stats = PS.A[p][0:1, 0:512]
                bcast = PS.b[p]
                if ev_guard[p]:
                    B.wait("y", evd, ev_guard[p])
                for c in range(DC):
                    B.dma("y", xf[p][c][:], src_h[seq, c * 128:(c + 1) * 128, ts], ld)
                ldv = B.mark(ld)
                B.wait("s", ld, ldv)
                for c in range(DC):
                    B.op("s", "activation", sq[p][c][:], xf[p][c][:], AF.Square, inc=(sqd, 1))
                B.wait("t", sqd, B.mark(sqd))
                if row_guard[p]:
                    B.wait("t", rowd, row_guard[p])
                for c in range(DC):
                    B.op("t", "matmul", stats, C.ones_col_bf[:], sq[p][c][:],
                         start=(c == 0), stop=(c == DC - 1),
                         inc=((mmd, 1) if c == DC - 1 else None))
                B.wait("v", mmd, B.mark(mmd))
                B.op("v", "tensor_scalar_mul", r_msq[:], stats, float(1.0 / D), inc=(rowd, 1))
                row_guard[p] = B.mark(rowd)
                B.wait("s", rowd, B.mark(rowd))
                B.op("s", "activation", r_std[:], r_msq[:], AF.Sqrt, bias=C.eps_rms[:, 0:1], inc=(rowd, 1))
                B.wait("v", rowd, B.mark(rowd))
                B.op("v", "reciprocal", r_rstdf[:], r_std[:])
                B.op("v", "tensor_copy", r_rstd[p][:], r_rstdf[:], inc=(rowd, 1))
                B.wait("t", rowd, B.mark(rowd))
                if ev_guard[p]:
                    B.wait("t", evd, ev_guard[p])
                B.op("t", "matmul", bcast[:], C.ones_row_bf[0:1, :], r_rstd[p][:], start=True, stop=True, inc=(bcd, 1))
                B.wait("v", bcd, B.mark(bcd))
                for c in range(DC):
                    buf = c % 2
                    if len(sp_marks) >= 2:
                        B.wait("v", spl, sp_marks[-2])
                    B.op("v", "scalar_tensor_tensor", xout[buf][:], xf[p][c][:], wcol[:, c:c + 1],
                         bcast[:], ALU.mult, ALU.mult, inc=(evd, 1))
                    B.wait("y", evd, B.mark(evd))
                    B.dma("y", dst_h[seq, c * 128:(c + 1) * 128, ts], xout[buf][:], spl)
                    sp_marks.append(B.mark(spl))
                ev_guard[p] = B.mark(evd)
                it += 1


def emit_attn_hd256(B, C, PS, tag, seq, src_h, win, bin1d, bin_c, ctx_dst):
    with ExitStack() as sc:
        xn_sb = B.sbuf(f"{tag}_xn", [128, 16384], BF16, sc)
        vt_sb = B.sbuf(f"{tag}_vt", [128, 16384], BF16, sc)
        wsb = B.sbuf(f"{tag}_w", [128, 8192], BF16, sc)
        wk_sb = B.sbuf(f"{tag}_wk", [128, 2048], BF16, sc)
        wq_sb = B.sbuf(f"{tag}_wq", [128, 2048], BF16, sc)
        qth = B.sbuf(f"{tag}_qth", [128, 2 * S], BF16, sc)
        kth = B.sbuf(f"{tag}_kth", [128, 2 * S], BF16, sc)
        probs = B.sbuf(f"{tag}_pr", [128, 16384], BF16, sc)
        bvbc = B.sbuf(f"{tag}_bvbc", [128, 1024], F32, sc)
        bvrow = B.sbuf(f"{tag}_bvrow", [1, 1024], BF16, sc)
        binc = B.sbuf(f"{tag}_binc", [128, 24], F32, sc)
        rec0 = B.sbuf(f"{tag}_rec", [1, QT], F32, sc)
        ctxu0 = B.sbuf(f"{tag}_cu0", [128, QT], BF16, sc)
        ctxu1 = B.sbuf(f"{tag}_cu1", [128, QT], BF16, sc)
        ctxo = [B.sbuf(f"{tag}_ctxo{i}", [128, QT], BF16, sc) for i in range(2)]
        ld, wl, mm, ex, pv, nr, ev, sp = "ld", "wl", "mm", "ex", "pv", "nr", "ev", "sp"
        ld_marks = {}
        for qq in range(NQT):
            for c in range(DC):
                B.dma("y", xn_sb[:, c * S + qq * QT:c * S + (qq + 1) * QT],
                      src_h[seq, c * 128:(c + 1) * 128, qq * QT:(qq + 1) * QT], ld)
            ld_marks[qq] = B.mark(ld)
        for c in range(DC):
            B.dma("g", wsb[:, c * 1024:(c + 1) * 1024],
                  win[c * 128:(c + 1) * 128, 2 * D:3 * D], wl)
        B.dma("g", bvrow[:], row_f32(bin1d, 2 * D, 3 * D), wl)
        B.dma("y", binc[:], bin_c, "bl")
        B.wait("t", wl, B.mark(wl))
        B.wait("v", "bl", B.mark("bl"))
        B.op("t", "matmul", PS.b[0][:], C.ones_row_bf[:], bvrow[:, 0:512], start=True, stop=True, inc=(mm, 1))
        B.op("t", "matmul", PS.b[1][:], C.ones_row_bf[:], bvrow[:, 512:1024], start=True, stop=True, inc=(mm, 1))
        B.wait("s", mm, B.mark(mm))
        B.op("s", "copy", bvbc[:, 0:512], PS.b[0][:], inc=(nr, 1))
        B.op("s", "copy", bvbc[:, 512:1024], PS.b[1][:], inc=(nr, 1))
        B.wait("t", nr, B.mark(nr))
        vev = []
        vi = 0
        for tt in range(16):
            for grp in range(2):
                bank = PS.b[vi % 2]
                if grp == 0:
                    B.wait("t", ld, ld_marks[tt // 4])
                if vi >= 2:
                    B.wait("t", ev, vev[vi - 2])
                for c in range(DC):
                    B.op("t", "matmul", bank[:],
                         xn_sb[:, c * S + tt * 128:c * S + (tt + 1) * 128],
                         wsb[:, c * 1024 + grp * 512:c * 1024 + (grp + 1) * 512],
                         start=(c == 0), stop=(c == DC - 1),
                         inc=((mm, 1) if c == DC - 1 else None))
                B.wait("v", mm, B.mark(mm))
                B.op("v", "tensor_tensor",
                     vt_sb[:, tt * 1024 + grp * 512:tt * 1024 + (grp + 1) * 512],
                     bank[:], bvbc[:, grp * 512:(grp + 1) * 512], ALU.add, inc=(ev, 1))
                vev.append(B.mark(ev))
                vi += 1
        exm = {}
        proj_mm_mark = B.mark(mm)
        proj_ev_mark = B.mark(ev)
        pv_mark = 0
        nr_mark = 0
        ev_mark = B.mark(ev)
        for hd in range(4):
            B.wait("g", mm, proj_mm_mark)
            for c in range(DC):
                B.dma("g", wq_sb[:, c * 256:(c + 1) * 256],
                      win[c * 128:(c + 1) * 128, hd * 256:(hd + 1) * 256], wl)
                B.dma("g", wk_sb[:, c * 256:(c + 1) * 256],
                      win[c * 128:(c + 1) * 128, D + hd * 256:D + (hd + 1) * 256], wl)
            wlv = B.mark(wl)
            bank_guard = {}
            for qt in range(NQT):
                ts = slice(qt * QT, (qt + 1) * QT)
                B.wait("t", wl, wlv)
                for cc in range(2):
                    bq_bank, bk_bank = (PS.b[0], PS.b[1]) if cc == 0 else (PS.b[2], PS.b[3])
                    if cc in bank_guard:
                        B.wait("t", ev, bank_guard[cc])
                    for c in range(DC):
                        B.op("t", "matmul", bq_bank[:], wq_sb[:, c * 256 + cc * 128:c * 256 + (cc + 1) * 128],
                             xn_sb[:, c * S + qt * QT:c * S + (qt + 1) * QT],
                             start=(c == 0), stop=(c == DC - 1),
                             inc=((mm, 1) if c == DC - 1 else None))
                    for c in range(DC):
                        B.op("t", "matmul", bk_bank[:], wk_sb[:, c * 256 + cc * 128:c * 256 + (cc + 1) * 128],
                             xn_sb[:, c * S + qt * QT:c * S + (qt + 1) * QT],
                             start=(c == 0), stop=(c == DC - 1),
                             inc=((mm, 1) if c == DC - 1 else None))
                    B.wait("v", mm, B.mark(mm))
                    B.op("v", "tensor_scalar_add", qth[:, cc * S + qt * QT:cc * S + (qt + 1) * QT],
                         bq_bank[:], binc[:, hd * 2 + cc:hd * 2 + cc + 1], inc=(ev, 1))
                    B.op("v", "tensor_scalar_add", kth[:, cc * S + qt * QT:cc * S + (qt + 1) * QT],
                         bk_bank[:], binc[:, 8 + hd * 2 + cc:8 + hd * 2 + cc + 1], inc=(ev, 1))
                    bank_guard[cc] = B.mark(ev)
            proj_mm_mark = B.mark(mm)
            proj_ev_mark = B.mark(ev)
            for qt in range(NQT):
                ts = slice(qt * QT, (qt + 1) * QT)
                for kc in range(KC):
                    buf = PS.A[(kc // 2) % 2]
                    half = kc % 2
                    if kc >= 4:
                        B.wait("t", ex, exm[(hd, qt, kc - 4)])
                    elif qt > 0:
                        B.wait("t", ex, exm[(hd, qt - 1, KC - 4 + kc)])
                    elif hd > 0:
                        B.wait("t", ex, exm[(hd - 1, NQT - 1, KC - 4 + kc)])
                    if kc == 0:
                        B.wait("t", ev, B.mark(ev))
                    for cc in range(2):
                        B.op("t", "matmul", buf[:, half * 512:(half + 1) * 512],
                             kth[:, cc * S + kc * 128:cc * S + (kc + 1) * 128],
                             qth[:, cc * S + qt * QT:cc * S + (qt + 1) * QT],
                             start=(cc == 0), stop=(cc == 1),
                             inc=((mm, 1) if cc == 1 else None))
                    if half == 1:
                        B.wait("s", mm, B.mark(mm))
                        if kc == 1 and pv_mark > 0:
                            B.wait("s", pv, pv_mark)
                        B.op("s", "activation", probs[:, (kc - 1) * QT:(kc + 1) * QT], buf[:],
                             AF.Exp, bias=C.negc[:, 0:1], scale=0.0625, inc=(ex, 1))
                        exm[(hd, qt, kc - 1)] = exm[(hd, qt, kc)] = B.mark(ex)
                for kc in range(KC):
                    B.wait("t", ex, exm[(hd, qt, kc)])
                    if kc == 0 and nr_mark:
                        B.wait("t", ev, nr_mark)
                    B.op("t", "matmul", PS.b[2][0:1, :], C.ones_col_bf[:],
                         probs[:, kc * QT:(kc + 1) * QT],
                         start=(kc == 0), stop=(kc == KC - 1), skip_group_check=True,
                         inc=((pv, 1) if kc == KC - 1 else None))
                B.wait("v", pv, B.mark(pv))
                B.op("v", "reciprocal", rec0[:], PS.b[2][0:1, :], inc=(nr, 1))
                recnr = B.mark(nr)
                for kc in range(KC):
                    if kc == 0 and ev_mark:
                        B.wait("t", ev, ev_mark)
                    for cc in range(2):
                        B.op("t", "matmul", PS.b[cc][:],
                             vt_sb[:, kc * 1024 + hd * 256 + cc * 128:kc * 1024 + hd * 256 + (cc + 1) * 128],
                             probs[:, kc * QT:(kc + 1) * QT],
                             start=(kc == 0), stop=(kc == KC - 1), skip_group_check=True,
                             inc=((pv, 1) if kc == KC - 1 else None))
                pv_mark = B.mark(pv)
                B.wait("v", pv, pv_mark)
                B.op("v", "tensor_copy", ctxu0[:], PS.b[0][:], lp=True)
                B.op("v", "tensor_copy", ctxu1[:], PS.b[1][:], lp=True, inc=(ev, 1))
                ev_mark = B.mark(ev)
                B.wait("t", nr, recnr)
                B.op("t", "matmul", PS.b[2][:], C.ones_row_f[:], rec0[:],
                     start=True, stop=True, inc=(mm, 1))
                B.wait("v", mm, B.mark(mm))
                B.op("v", "tensor_tensor", ctxo[0][:], ctxu0[:], PS.b[2][:], ALU.mult, inc=(ev, 1))
                B.op("v", "tensor_tensor", ctxo[1][:], ctxu1[:], PS.b[2][:], ALU.mult, inc=(ev, 1))
                nr_mark = B.mark(ev)
                B.wait("y", ev, B.mark(ev))
                B.dma("y", ctx_dst[seq, hd * 256:hd * 256 + 128, ts], ctxo[0][:], sp)
                B.dma("y", ctx_dst[seq, hd * 256 + 128:hd * 256 + 256, ts], ctxo[1][:], sp)


def emit_comb(B, C, PS, H, combT_sb):
    with ExitStack() as sc:
        gw = B.sbuf("cb_gw", [128, 64], BF16, sc)
        yt = [B.sbuf(f"cb_y{c}", [128, QT], BF16, sc) for c in range(DC)]
        m1 = B.sbuf("cb_m1", [128, 1], F32, sc)
        m2 = B.sbuf("cb_m2", [128, 1], F32, sc)
        msk1 = B.sbuf("cb_msk1", [128, 8], F32, sc)
        msk2 = B.sbuf("cb_msk2", [128, 8], F32, sc)
        l2 = B.sbuf("cb_l2", [128, 8], F32, sc)
        dcol = B.sbuf("cb_dcol", [128, 1], F32, sc)
        s1c = B.sbuf("cb_s1c", [128, 1], F32, sc)
        s2c = B.sbuf("cb_s2c", [128, 1], F32, sc)
        comb = B.sbuf("cb_comb", [128, 8], F32, sc)
        combb = B.sbuf("cb_combb", [128, 8], BF16, sc)
        for c in range(DC):
            B.dma("g", gw[:, c * 8:(c + 1) * 8], H["l0f_gatewT"][c * 128:(c + 1) * 128, :], "wl")
        wlv = B.mark("wl")
        for seq in range(SEQ):
            for qt in range(NQT):
                ts = slice(qt * QT, (qt + 1) * QT)
                if seq + qt > 0:
                    B.wait("y", "mm", B.mark("mm"))
                for c in range(DC):
                    B.dma("y", yt[c][:], H["y_h"][seq, c * 128:(c + 1) * 128, ts], "ld")
                ldv = B.mark("ld")
                B.wait("t", "ld", ldv)
                B.wait("t", "wl", wlv)
                for tt in range(4):
                    bank = PS.b[tt % 2]
                    B.wait("t", "ev", B.mark("ev"))
                    for c in range(DC):
                        B.op("t", "matmul", bank[0:128, 0:8],
                             yt[c][:, tt * 128:(tt + 1) * 128], gw[:, c * 8:(c + 1) * 8],
                             start=(c == 0), stop=(c == DC - 1),
                             inc=(("mm", 1) if c == DC - 1 else None))
                    B.wait("v", "mm", B.mark("mm"))
                    B.op("v", "reduce_max", m1[:], bank[0:128, 0:8], axis=AX.X, inc=("nr", 1))
                    B.wait("v", "nr", B.mark("nr"))
                    B.op("v", "tensor_scalar", msk1[:], bank[0:128, 0:8], m1[:, 0:1], None, ALU.is_ge, inc=("nr", 1))
                    B.wait("v", "nr", B.mark("nr"))
                    B.op("v", "scalar_tensor_tensor", l2[:], msk1[:], -1e30, bank[0:128, 0:8],
                         ALU.mult, ALU.add, inc=("nr", 1))
                    B.wait("v", "nr", B.mark("nr"))
                    B.op("v", "reduce_max", m2[:], l2[:], axis=AX.X, inc=("nr", 1))
                    B.wait("v", "nr", B.mark("nr"))
                    B.op("v", "tensor_scalar", msk2[:], l2[:], m2[:, 0:1], None, ALU.is_ge)
                    B.op("v", "tensor_tensor", dcol[:], m2[:], m1[:], ALU.subtract, inc=("nr", 1))
                    B.wait("s", "nr", B.mark("nr"))
                    B.op("s", "activation", s2c[:], dcol[:], AF.Sigmoid)
                    B.op("s", "activation", s1c[:], dcol[:], AF.Sigmoid, scale=-1.0, inc=("nr", 1))
                    B.wait("v", "nr", B.mark("nr"))
                    B.op("v", "tensor_scalar_mul", comb[:], msk2[:], s2c[:, 0:1], inc=("nr", 1))
                    B.wait("v", "nr", B.mark("nr"))
                    B.op("v", "scalar_tensor_tensor", comb[:], msk1[:], s1c[:, 0:1], comb[:],
                         ALU.mult, ALU.add, inc=("nr", 1))
                    B.wait("v", "nr", B.mark("nr"))
                    B.op("v", "tensor_copy", combb[:], comb[:], inc=("ev", 1))
                    B.wait("t", "ev", B.mark("ev"))
                    B.wait("t", "nr", B.mark("nr"))
                    B.op("t", "matmul", PS.b[2][0:8, 0:128], combb[:], C.ident[:],
                         start=True, stop=True, inc=("mm", 1))
                    B.wait("v", "mm", B.mark("mm"))
                    B.op("v", "tensor_copy", combT_sb[seq][:, qt * QT + tt * 128:qt * QT + (tt + 1) * 128],
                         PS.b[2][0:8, 0:128], inc=("nr", 1))
                    if DEV:
                        B.wait("g", "nr", B.mark("nr"))
                        B.dma("g", H["combT_h"][seq, :, qt * QT + tt * 128:qt * QT + (tt + 1) * 128],
                              combT_sb[seq][:, qt * QT + tt * 128:qt * QT + (tt + 1) * 128], "sp")


def emit_moe(B, C, PS, H, combT_sb):
    with ExitStack() as sc:
        w1s = B.sbuf("mo_w1", [128, 16384], BF16, sc)
        w2s = B.sbuf("mo_w2", [128, 16384], BF16, sc)
        w3s = B.sbuf("mo_w3", [128, 16384], BF16, sc)
        acc = B.sbuf("mo_acc", [128, 16384], BF16, sc)
        h_sb = B.sbuf("mo_h", [128, 8192], BF16, sc)
        yt = [[B.sbuf(f"mo_y{b}_{c}", [128, QT], BF16, sc) for c in range(DC)] for b in range(2)]
        silu_sb = [B.sbuf(f"mo_sl{i}", [128, QT], F32, sc) for i in range(2)]
        combbc = B.sbuf("mo_cbc", [128, QT], F32, sc)
        b1cb = [B.sbuf(f"mo_b1c{i}", [128, 16], F32, sc) for i in range(2)]
        b2cb = [B.sbuf(f"mo_b2c{i}", [128, 16], F32, sc) for i in range(2)]
        b3cb = [B.sbuf(f"mo_b3c{i}", [128, 8], F32, sc) for i in range(2)]
        tmp = B.sbuf("mo_tmp", [128, QT], F32, sc)
        tx1 = [B.sbuf(f"mo_tx1{i}", [128, QT], BF16, sc) for i in range(2)]
        tss = [B.sbuf(f"mo_ts{i}", [128, QT], BF16, sc) for i in range(2)]
        txx = B.sbuf("mo_tx", [128, QT], F32, sc)
        tg = B.sbuf("mo_tg", [128, QT], BF16, sc)
        tout = B.sbuf("mo_to", [128, QT], F32, sc)

        def load_w(e, which, slot, sem="wl"):
            src = H[f"l0f_w{which}T"]
            n = 2048 if which in (1, 2) else 1024
            kc_n = 8 if which in (1, 2) else 16
            for kc in range(kc_n):
                B.dma("g", slot[:, kc * n:(kc + 1) * n], src[e, kc * 128:(kc + 1) * 128, :], sem)

        def load_bias_cols(e, i):
            B.dma("y", b1cb[i % 2][:], H["l0f_b1_c"][e], "bl")
            B.dma("y", b2cb[i % 2][:], H["l0f_b2_c"][e], "bl")
            B.dma("y", b3cb[i % 2][:], H["l0f_b3_c"][e], "bl")

        ybuf_guard = [0, 0]
        load_w(0, 1, w1s)
        load_w(0, 2, w2s)
        load_w(0, 3, w3s)
        tail_sp = []
        tail_ev = []
        tp2 = tmp
        bias_guard = [0, 0]
        prev_yev = 0
        prev_ynr = 0
        for seq in range(SEQ):
            for e in range(8):
                ei = seq * 8 + e
                b1c, b2c, b3c = b1cb[ei % 2], b2cb[ei % 2], b3cb[ei % 2]
                if bias_guard[ei % 2]:
                    B.wait("y", "ev", bias_guard[ei % 2])
                load_bias_cols(e, ei)
                wlv = B.mark("wl")
                blv = B.mark("bl")
                h_marks = []
                for qt in range(NQT):
                    ts = slice(qt * QT, (qt + 1) * QT)
                    ybuf = yt[qt % 2]
                    if ybuf_guard[qt % 2]:
                        B.wait("y", "mm", ybuf_guard[qt % 2])
                    for c in range(DC):
                        B.dma("y", ybuf[c][:], H["y_h"][seq, c * 128:(c + 1) * 128, ts], "ld")
                    ldv = B.mark("ld")
                    B.wait("t", "nr", B.mark("nr"))
                    B.op("t", "matmul", PS.b[2][:], C.esel[e][:], combT_sb[seq][:, ts],
                         start=True, stop=True, inc=("mm", 1))
                    B.wait("s", "mm", B.mark("mm"))
                    B.op("s", "copy", combbc[:], PS.b[2][:], inc=("nr", 1))
                    B.wait("t", "ld", ldv)
                    B.wait("t", "wl", wlv)
                    B.wait("s", "bl", blv)
                    B.wait("v", "bl", blv)
                    if prev_yev:
                        B.wait("t", "ev", prev_yev)
                        B.wait("t", "nr", prev_ynr)
                    sst_marks = []
                    silu_marks = []
                    for hc in range(16):
                        hb1 = PS.b[0] if hc % 2 == 0 else PS.b[2]
                        hb2 = PS.b[1] if hc % 2 == 0 else PS.b[3]
                        if hc == 1:
                            B.wait("t", "nr", B.mark("nr"))
                        if hc >= 2:
                            B.wait("t", "nr", silu_marks[hc - 2])
                        for kc in range(DC):
                            B.op("t", "matmul", hb1[:],
                                 w1s[:, kc * 2048 + hc * 128:kc * 2048 + (hc + 1) * 128],
                                 ybuf[kc][:], start=(kc == 0), stop=(kc == DC - 1),
                                 inc=(("mm", 1) if kc == DC - 1 else None))
                        B.wait("s", "mm", B.mark("mm"))
                        if hc >= 2:
                            B.wait("s", "ev", sst_marks[hc - 2])
                        B.op("s", "activation", silu_sb[hc % 2][:], hb1[:], AF.Silu,
                             bias=b1c[:, hc:hc + 1], inc=("nr", 1))
                        snr = B.mark("nr")
                        silu_marks.append(snr)
                        if hc >= 2:
                            B.wait("t", "ev", sst_marks[hc - 2])
                        for kc in range(DC):
                            B.op("t", "matmul", hb2[:],
                                 w2s[:, kc * 2048 + hc * 128:kc * 2048 + (hc + 1) * 128],
                                 ybuf[kc][:], start=(kc == 0), stop=(kc == DC - 1),
                                 inc=(("mm", 1) if kc == DC - 1 else None))
                        B.wait("v", "mm", B.mark("mm"))
                        B.wait("v", "nr", snr)
                        B.op("v", "scalar_tensor_tensor", h_sb[:, hc * QT:(hc + 1) * QT],
                             hb2[:], b2c[:, hc:hc + 1], silu_sb[hc % 2][:],
                             ALU.add, ALU.mult, inc=("ev", 1))
                        sst_marks.append(B.mark("ev"))
                    hdone = B.mark("ev")
                    h_marks.append(B.mark("mm"))
                    ybuf_guard[qt % 2] = B.mark("mm")
                    yev_marks = []
                    hsilu_done = B.mark("nr")
                    for m in range(DC):
                        bank = PS.b[m % 4]
                        if m == 0:
                            B.wait("t", "nr", hsilu_done)
                        B.wait("t", "ev", hdone if m < 4 else yev_marks[m - 4])
                        for kc in range(16):
                            B.op("t", "matmul", bank[:],
                                 w3s[:, kc * 1024 + m * 128:kc * 1024 + (m + 1) * 128],
                                 h_sb[:, kc * QT:(kc + 1) * QT],
                                 start=(kc == 0), stop=(kc == 15),
                                 inc=(("mm", 1) if kc == 15 else None))
                        B.wait("v", "mm", B.mark("mm"))
                        aslice = acc[:, m * 2048 + qt * QT:m * 2048 + (qt + 1) * QT]
                        if e == 0:
                            B.op("v", "scalar_tensor_tensor", aslice, bank[:],
                                 b3c[:, m:m + 1], combbc[:], ALU.add, ALU.mult, inc=("ev", 1))
                        else:
                            B.op("v", "scalar_tensor_tensor", tmp[:], bank[:],
                                 b3c[:, m:m + 1], combbc[:], ALU.add, ALU.mult, inc=("ev", 1))
                            B.wait("v", "ev", B.mark("ev"))
                            B.op("v", "tensor_tensor", aslice, aslice, tmp[:], ALU.add, inc=("ev", 1))
                        yev_marks.append(B.mark("ev"))
                        if e == 7:
                            c = m
                            bufi = c % 2
                            if len(tail_sp) >= 2:
                                B.wait("g", "gsp", tail_sp[-2])
                            B.dma("g", tx1[bufi][:], H["x1_h"][seq, c * 128:(c + 1) * 128, ts], "gld")
                            B.dma("g", tss[bufi][:], H["s_h"][seq, c * 128:(c + 1) * 128, ts], "gld")
                            if tail_ev:
                                B.wait("g", "ev", tail_ev[-1])
                            B.dma("g", txx[:], H["xT"][seq, c * 128:(c + 1) * 128, ts], "gld")
                            ldv2 = B.mark("gld")
                            B.wait("v", "gld", ldv2)
                            B.op("v", "tensor_tensor", tp2[:], tx1[bufi][:], aslice, ALU.add, inc=("nr", 1))
                            B.wait("s", "nr", B.mark("nr"))
                            B.op("s", "activation", tg[:], tp2[:], AF.Silu, inc=("nr", 1))
                            B.wait("v", "nr", B.mark("nr"))
                            B.op("v", "tensor_tensor", tp2[:], tss[bufi][:], tg[:], ALU.mult)
                            if tail_sp:
                                B.wait("v", "gsp", tail_sp[-1])
                            B.op("v", "tensor_tensor", tout[:], txx[:], tp2[:], ALU.add, inc=("ev", 1))
                            tail_ev.append(B.mark("ev"))
                            B.wait("g", "ev", B.mark("ev"))
                            B.dma("g", H["xr_h"][seq, c * 128:(c + 1) * 128, ts], tout[:], "gsp")
                            tail_sp.append(B.mark("gsp"))
                    prev_yev = B.mark("ev")
                    prev_ynr = B.mark("nr")
                    if qt == NQT - 1:
                        bias_guard[ei % 2] = B.mark("ev")
                        if not (seq == SEQ - 1 and e == 7):
                            ne = (e + 1) % 8
                            B.wait("g", "mm", h_marks[-1])
                            load_w(ne, 1, w1s)
                            load_w(ne, 2, w2s)
                            B.wait("g", "mm", B.mark("mm"))
                            load_w(ne, 3, w3s)



def emit_ffn1(B, C, PS, H):
    HC1 = 24
    with ExitStack() as sc:
        wg_sb = B.sbuf("f1_wg", [128, 24576], BF16, sc)
        wu_sb = B.sbuf("f1_wu", [128, 24576], BF16, sc)
        wd_sb = B.sbuf("f1_wd", [128, 24576], BF16, sc)
        h_sb = B.sbuf("f1_h", [128, HC1 * QT], BF16, sc)
        xt = [[B.sbuf(f"f1_x{b}_{c}", [128, QT], BF16, sc) for c in range(DC)] for b in range(2)]
        silu_sb = [B.sbuf(f"f1_sl{i}", [128, QT], BF16, sc) for i in range(2)]
        tg = B.sbuf("f1_tg", [128, QT], BF16, sc)
        tp = B.sbuf("f1_tp", [128, QT], F32, sc)
        tss = B.sbuf("f1_ts", [128, QT], BF16, sc)
        txx = B.sbuf("f1_txx", [128, QT], F32, sc)
        tout = B.sbuf("f1_to", [128, QT], F32, sc)
        for kc in range(DC):
            B.dma("g", wg_sb[:, kc * 3072:(kc + 1) * 3072], H["l1f_wgT"][kc * 128:(kc + 1) * 128, :], "wl")
            B.dma("g", wu_sb[:, kc * 3072:(kc + 1) * 3072], H["l1f_wuT"][kc * 128:(kc + 1) * 128, :], "wl")
        for kc in range(HC1):
            B.dma("g", wd_sb[:, kc * 1024:(kc + 1) * 1024], H["l1f_wdT"][kc * 128:(kc + 1) * 128, :], "wl")
        wlv = B.mark("wl")
        sp_marks = []
        xt_guard = [0, 0]
        f_prev_yev = 0
        f_prev_ynr = 0
        it = 0
        for seq in range(SEQ):
            for qt in range(NQT):
                ts = slice(qt * QT, (qt + 1) * QT)
                xc = xt[it % 2]
                if xt_guard[it % 2]:
                    B.wait("y", "mm", xt_guard[it % 2])
                for c in range(DC):
                    B.dma("y", xc[c][:], H["xn_h"][seq, c * 128:(c + 1) * 128, ts], "ld")
                ldv = B.mark("ld")
                B.wait("t", "ld", ldv)
                B.wait("t", "wl", wlv)
                if f_prev_yev:
                    B.wait("t", "ev", f_prev_yev)
                    B.wait("t", "nr", f_prev_ynr)
                sst_marks = []
                silu_marks = []
                for hc in range(HC1):
                    hb1 = PS.b[0] if hc % 2 == 0 else PS.b[2]
                    hb2 = PS.b[1] if hc % 2 == 0 else PS.b[3]
                    if hc >= 2:
                        B.wait("t", "nr", silu_marks[hc - 2])
                    for kc in range(DC):
                        B.op("t", "matmul", hb1[:],
                             wg_sb[:, kc * 3072 + hc * 128:kc * 3072 + (hc + 1) * 128],
                             xc[kc][:], start=(kc == 0), stop=(kc == DC - 1),
                             inc=(("mm", 1) if kc == DC - 1 else None))
                    B.wait("s", "mm", B.mark("mm"))
                    if hc >= 2:
                        B.wait("s", "ev", sst_marks[hc - 2])
                    B.op("s", "activation", silu_sb[hc % 2][:], hb1[:], AF.Silu,
                         scale=float(1.0 / 127.0), inc=("nr", 1))
                    snr = B.mark("nr")
                    silu_marks.append(snr)
                    if hc >= 2:
                        B.wait("t", "ev", sst_marks[hc - 2])
                    for kc in range(DC):
                        B.op("t", "matmul", hb2[:],
                             wu_sb[:, kc * 3072 + hc * 128:kc * 3072 + (hc + 1) * 128],
                             xc[kc][:], start=(kc == 0), stop=(kc == DC - 1),
                             inc=(("mm", 1) if kc == DC - 1 else None))
                    B.wait("v", "mm", B.mark("mm"))
                    B.wait("v", "nr", snr)
                    B.op("v", "scalar_tensor_tensor", h_sb[:, hc * QT:(hc + 1) * QT],
                         hb2[:], float(1.0 / 127.0), silu_sb[hc % 2][:],
                         ALU.mult, ALU.mult, inc=("ev", 1))
                    sst_marks.append(B.mark("ev"))
                hdone = B.mark("ev")
                xt_guard[it % 2] = B.mark("mm")
                it += 1
                fev_marks = []
                fsilu_done = B.mark("nr")
                for m in range(DC):
                    bank = PS.b[m % 4]
                    if m == 0:
                        B.wait("t", "nr", fsilu_done)
                    B.wait("t", "ev", hdone if m < 4 else fev_marks[m - 4])
                    for kc in range(HC1):
                        B.op("t", "matmul", bank[:],
                             wd_sb[:, kc * 1024 + m * 128:kc * 1024 + (m + 1) * 128],
                             h_sb[:, kc * QT:(kc + 1) * QT],
                             start=(kc == 0), stop=(kc == HC1 - 1),
                             inc=(("mm", 1) if kc == HC1 - 1 else None))
                    B.wait("y", "ev", B.mark("ev"))
                    B.dma("y", tss[:], H["s_h"][seq, m * 128:(m + 1) * 128, ts], "ld")
                    B.dma("y", txx[:], H["xr_h"][seq, m * 128:(m + 1) * 128, ts], "ld")
                    B.wait("s", "mm", B.mark("mm"))
                    B.wait("s", "ev", B.mark("ev"))
                    B.op("s", "activation", tg[:], bank[:], AF.Silu,
                         scale=float(1.0 / 127.0), inc=("nr", 1))
                    B.wait("v", "nr", B.mark("nr"))
                    B.wait("v", "ld", B.mark("ld"))
                    B.op("v", "tensor_tensor", tp[:], tss[:], tg[:], ALU.mult)
                    if sp_marks:
                        B.wait("v", "sp", sp_marks[-1])
                    B.op("v", "tensor_tensor", tout[:], txx[:], tp[:], ALU.add, inc=("ev", 1))
                    fev_marks.append(B.mark("ev"))
                    B.wait("y", "ev", B.mark("ev"))
                    B.dma("y", H["outT"][seq, m * 128:(m + 1) * 128, ts], tout[:], "sp")
                    sp_marks.append(B.mark("sp"))
                    if m == DC - 1:
                        f_prev_yev = B.mark("ev")
                        f_prev_ynr = B.mark("nr")


def host_layout(inputs):
    t = lambda a: np.ascontiguousarray(a)
    shared = {
        "l0a_wq": t(inputs["L0_attn_Wq"]), "l0a_bq": t(inputs["L0_attn_bq"]),
        "l0a_wk": t(inputs["L0_attn_Wk"]), "l0a_bk": t(inputs["L0_attn_bk"]),
        "l0a_wv": t(inputs["L0_attn_Wv"]), "l0a_bv": t(inputs["L0_attn_bv"]),
        "l0a_cgw": t(inputs["L0_attn_cg_W"]), "l0a_cgb": t(inputs["L0_attn_cg_b"]),
        "l0a_amw": t(inputs["L0_attn_am_W"]), "l0a_amb": t(inputs["L0_attn_am_b"]),
        "l0a_cons": t(inputs["L0_attn_cons_vec"]),
        "l0a_cawin": t(inputs["L0_attn_ca_Win"]), "l0a_cabin": t(inputs["L0_attn_ca_bin"]),
        "l0a_cawout": t(inputs["L0_attn_ca_Wout"]), "l0a_cabout": t(inputs["L0_attn_ca_bout"]),
        "l0a_mcwin": t(inputs["L0_attn_mc_Win"]), "l0a_mcbin": t(inputs["L0_attn_mc_bin"]),
        "l0a_mcwout": t(inputs["L0_attn_mc_Wout"]), "l0a_mcbout": t(inputs["L0_attn_mc_bout"]),
        "l0a_wo": t(inputs["L0_attn_Wo"]), "l0a_bo": t(inputs["L0_attn_bo"]),
        "l0f_n1w": t(inputs["L0_ffn_norm1_w"]), "l0f_n2w": t(inputs["L0_ffn_norm2_w"]),
        "l0f_qkvwT": t(inputs["L0_ffn_qkv_w"].T), "l0f_qkvb": t(inputs["L0_ffn_qkv_b"]),
        "l0f_outwT": t(inputs["L0_ffn_out_w"].T), "l0f_outb": t(inputs["L0_ffn_out_b"]),
        "l0f_gatewT": t(inputs["L0_ffn_gate_w"].T),
        "l0f_w1T": t(inputs["L0_ffn_w1"].transpose(0, 2, 1)), "l0f_b1": t(inputs["L0_ffn_b1"]),
        "l0f_w2T": t(inputs["L0_ffn_w2"].transpose(0, 2, 1)), "l0f_b2": t(inputs["L0_ffn_b2"]),
        "l0f_w3T": t(inputs["L0_ffn_w3"].transpose(0, 2, 1)), "l0f_b3": t(inputs["L0_ffn_b3"]),
        "l1a_wqT": t(inputs["L1_attn_Wq"].T), "l1a_bq": t(inputs["L1_attn_bq"]),
        "l1a_wkT": t(inputs["L1_attn_Wk"].T), "l1a_bk": t(inputs["L1_attn_bk"]),
        "l1a_wvT": t(inputs["L1_attn_Wv"].T), "l1a_bv": t(inputs["L1_attn_bv"]),
        "l1a_woT": t(inputs["L1_attn_Wo"].T), "l1a_bo": t(inputs["L1_attn_bo"]),
        "l1f_wgT": t(inputs["L1_ffn_w_gate"].T), "l1f_wuT": t(inputs["L1_ffn_w_up"].T),
        "l1f_wdT": t(inputs["L1_ffn_w_down"].T),
    }
    cols = lambda a: np.ascontiguousarray(np.asarray(a).reshape(-1, 128).T)
    ecols = lambda a: np.ascontiguousarray(np.stack([cols(x) for x in a]))
    shared.update({
        "l0a_bq_c": cols(inputs["L0_attn_bq"]), "l0a_bk_c": cols(inputs["L0_attn_bk"]),
        "l0a_cabin_c": cols(inputs["L0_attn_ca_bin"]), "l0a_mcbin_c": cols(inputs["L0_attn_mc_bin"]),
        "l0a_cabout_c": cols(inputs["L0_attn_ca_bout"]), "l0a_mcbout_c": cols(inputs["L0_attn_mc_bout"]),
        "l0a_bo_c": cols(inputs["L0_attn_bo"]),
        "l0f_qkvb_c": cols(inputs["L0_ffn_qkv_b"]), "l0f_outb_c": cols(inputs["L0_ffn_out_b"]),
        "l0f_b1_c": ecols(inputs["L0_ffn_b1"]), "l0f_b2_c": ecols(inputs["L0_ffn_b2"]),
        "l0f_b3_c": ecols(inputs["L0_ffn_b3"]),
        "l0f_n1w_c": cols(inputs["L0_ffn_norm1_w"]), "l0f_n2w_c": cols(inputs["L0_ffn_norm2_w"]),
        "l1a_bq_c": cols(inputs["L1_attn_bq"]), "l1a_bk_c": cols(inputs["L1_attn_bk"]),
        "l1a_bo_c": cols(inputs["L1_attn_bo"]),
    })
    x = inputs["L0_attn_hidden_states"]
    in_maps = []
    for c in range(NCORE):
        m = dict(shared)
        m["xT"] = t(x[c * SEQ:(c + 1) * SEQ].transpose(0, 2, 1))
        in_maps.append(m)
    return in_maps


class PSums:
    pass


def build_program(phases):
    nc = bass.Bass("TRN2", target_bir_lowering=False)
    B = Builder(nc)
    H = declare_hbm(nc)
    PS = PSums()
    PS.A = [B.psum(f"psA{i}", [128, 1024], F32) for i in range(2)]
    PS.b = [B.psum(f"pb{i}", [128, 512], F32) for i in range(4)]
    C = emit_consts(B, PS)
    B.phase("consts")
    scopes = {}
    modT_sb = [None, None]
    COMBT = [None, None]

    def open_modT():
        scopes["mod"] = ExitStack()
        for s_ in range(SEQ):
            modT_sb[s_] = B.sbuf(f"modT{s_}", [16, S], BF16, scopes["mod"])

    def open_combT():
        scopes["comb"] = ExitStack()
        for s_ in range(SEQ):
            COMBT[s_] = B.sbuf(f"combT{s_}", [8, S], BF16, scopes["comb"])

    for ph in phases:
        if ph == "ln0":
            emit_layernorm(B, C, PS, H["xT"], H["xn_h"], "ln0")
        elif ph == "mod0":
            open_modT()
            emit_mod(B, C, PS, H, modT_sb)
        elif ph == "attn0":
            for seq in range(SEQ):
                emit_attn_pairs(B, C, PS, H, f"a0s{seq}", seq, H["xn_h"],
                                H["l0a_wq"], H["l0a_bq_c"], H["l0a_wk"], H["l0a_bk_c"],
                                H["l0a_wv"], H["l0a_bv"], H["ctxa_h"], modT=modT_sb)
                B.phase(f"attn0s{seq}")
            scopes.pop("mod").close()
        elif ph == "ca0":
            for seq in range(SEQ):
                emit_attn_hd256(B, C, PS, f"c0s{seq}", seq, H["xn_h"],
                                H["l0a_cawin"], H["l0a_cabin"], H["l0a_cabin_c"], H["ctxb_h"])
                B.phase(f"ca0s{seq}")
        elif ph == "camix":
            emit_proj(B, C, PS, "cax", H["ctxb_h"], H["ctxb_h"], H["l0a_cawout"],
                      H["l0a_cabout_c"], mode="mix", res_h=H["ctxa_h"], w_res=0.3, w_new=0.7)
        elif ph == "mc0":
            for seq in range(SEQ):
                emit_attn_hd256(B, C, PS, f"m0s{seq}", seq, H["ctxb_h"],
                                H["l0a_mcwin"], H["l0a_mcbin"], H["l0a_mcbin_c"], H["ctxa_h"])
                B.phase(f"mc0s{seq}")
        elif ph == "mcmix":
            emit_proj(B, C, PS, "mcx", H["ctxa_h"], H["ctxa_h"], H["l0a_mcwout"],
                      H["l0a_mcbout_c"], mode="mix", res_h=H["ctxb_h"], w_res=0.85, w_new=0.15)
        elif ph == "wo0":
            emit_proj(B, C, PS, "wo0", H["ctxa_h"], H["s_h"], H["l0a_wo"],
                      H["l0a_bo_c"], mode="bias")
        elif ph == "rms1":
            emit_rms(B, C, PS, "rm1", H["xn_h"], H["y_h"], H["l0f_n1w_c"])
        elif ph == "ropeattn":
            for seq in range(SEQ):
                emit_attn_pairs(B, C, PS, H, f"rp{seq}", seq, H["y_h"],
                                H["l0f_qkvwT"][:, 0:D], H["l0f_qkvb_c"][:, 0:8],
                                H["l0f_qkvwT"][:, D:2 * D], H["l0f_qkvb_c"][:, 8:16],
                                H["l0f_qkvwT"][:, 2 * D:3 * D], H["l0f_qkvb"][2 * D:3 * D],
                                H["ctxb_h"], rope=True)
                B.phase(f"ropeattn{seq}")
        elif ph == "ropeproj":
            emit_proj(B, C, PS, "rpp", H["ctxb_h"], H["x1_h"], H["l0f_outwT"],
                      H["l0f_outb_c"], mode="bias_res", res_h=H["xn_h"])
        elif ph == "rms2":
            emit_rms(B, C, PS, "rm2", H["x1_h"], H["y_h"], H["l0f_n2w_c"])
        elif ph == "comb":
            open_combT()
            emit_comb(B, C, PS, H, COMBT)
        elif ph == "moe":
            emit_moe(B, C, PS, H, COMBT)
            scopes.pop("comb").close()
        elif ph == "ln1":
            emit_layernorm(B, C, PS, H["xr_h"], H["xn_h"], "ln1")
        elif ph == "attn1":
            for seq in range(SEQ):
                emit_attn_pairs(B, C, PS, H, f"a1s{seq}", seq, H["xn_h"],
                                H["l1a_wqT"], H["l1a_bq_c"], H["l1a_wkT"], H["l1a_bk_c"],
                                H["l1a_wvT"], H["l1a_bv"], H["ctxa_h"])
                B.phase(f"attn1s{seq}")
        elif ph == "wo1":
            emit_proj(B, C, PS, "wo1", H["ctxa_h"], H["s_h"], H["l1a_woT"],
                      H["l1a_bo_c"], mode="bias")
        elif ph == "ffn1":
            emit_ffn1(B, C, PS, H)
        else:
            raise ValueError(f"phase {ph} not implemented yet")
        B.phase(ph)

    emit_epilogue(B, H)
    B.finalize()
    return nc, B, H


def emit_epilogue(B, H):
    for sname, val in list(B.ctr.items()):
        if val > 0:
            B.wait("g", sname, val)


def kernel(**inputs):
    in_maps = host_layout(inputs)
    nc, B, H = build_program([p for p in ALL_PHASES])
    res = run_bass_kernel_spmd(nc, in_maps, core_ids=list(range(NCORE)))
    outs = []
    for c in range(NCORE):
        o = np.asarray(res.results[c]["outT"])
        outs.append(o.transpose(0, 2, 1))
    return np.ascontiguousarray(np.concatenate(outs, axis=0), dtype=np.float32)
